# version 12
# baseline (speedup 1.0000x reference)
import numpy as np
from contextlib import ExitStack
import concourse.bass as bass
import concourse.mybir as mybir

F32 = mybir.dt.float32
BF16 = mybir.dt.bfloat16
F16 = mybir.dt.float16
AF = mybir.ActivationFunctionType
ALU = mybir.AluOpType
AX = mybir.AxisListType

T = 4096
DM = 1024
NIC = 8
NTC = 32
NQT = 8
NH = 16
HD = 64
NKV = 8
NE = 34
NRE = 32
EI = 512
OFF = 3968
DW = 8192
QT1 = 1024
NQ1 = T // QT1
NKC = T // 128
NEG = -1e30
ECONST = 4.0
HEAD_PAIRS = [(0, 2), (1, 3), (4, 6), (5, 7), (8, 10), (9, 11), (12, 14), (13, 15)]


class MK:
    def __init__(self, stop_after=None, debug=False):
        self.stop_after = stop_after
        self.debug = debug
        nc = bass.Bass("TRN2", target_bir_lowering=False)
        self.nc = nc
        d = nc.dram_tensor
        scr_kind = "ExternalOutput" if debug else "Internal"
        self.x = d("x", [T, DM], F32, kind="ExternalInput").ap()
        self.wqT = d("wqT", [DM, DM], F32, kind="ExternalInput").ap()
        self.wkT = d("wkT", [DM, 512], F32, kind="ExternalInput").ap()
        self.wvT = d("wvT", [DM, 512], F32, kind="ExternalInput").ap()
        self.woT = d("woT", [DM, DM], F32, kind="ExternalInput").ap()
        self.bo = d("bo", [DM, 1], F32, kind="ExternalInput").ap()
        self.gwT = d("gwT", [DM, 32], F32, kind="ExternalInput").ap()
        self.ecorr = d("ecorr", [1, 32], F32, kind="ExternalInput").ap()
        self.sel_in = d("sel_in", [32, NE * 128], BF16, kind="ExternalInput").ap()
        self.guT = d("guT", [NE, DM, 1024], F32, kind="ExternalInput").ap()
        self.dnT = d("dnT", [NE, EI, DM], F32, kind="ExternalInput").ap()
        self.out = d("out", [T, DM], F32, kind="ExternalOutput").ap()
        self.xn_scr = d("xn_scr", [NQT, NIC, 128, 512], BF16, kind=scr_kind).ap()
        self.o_scr = d("o_scr", [NQ1, NIC, 128, QT1], BF16, kind=scr_kind).ap()
        if debug:
            self.comb_dbg = d("comb_dbg", [32, T], F32, kind="ExternalOutput").ap()

        self.ctx = ExitStack()
        self.sems = {}
        self.n_vsc = 0
        self.n_sedge = 0

    def sb(self, ctx, name, shape, dtype, side="left"):
        return ctx.enter_context(self.nc.sbuf_tensor(name, shape, dtype, side=side))

    def ps(self, ctx, name, shape, dtype=F32):
        return ctx.enter_context(self.nc.psum_tensor(name, shape, dtype))

    def sedge(self, s, instr):
        self.n_sedge += 1
        instr.then_inc(self.s_ostr, 1)
        s.wait_ge(self.s_ostr, self.n_sedge)

    def vedge(self, v, instr):
        self.n_vsc += 1
        instr.then_inc(self.s_vsc, 1)
        v.wait_ge(self.s_vsc, self.n_vsc)

    def sem(self, name):
        s = self.ctx.enter_context(self.nc.semaphore(name))
        self.sems[name] = s
        return s

    def build(self):
        nc = self.nc
        c = self.ctx
        self.ident_bf = self.sb(c, "ident_bf", [128, 128], BF16)
        self.ident_f = self.sb(c, "ident_f", [128, 128], F32)
        self.ones_r = self.sb(c, "ones_r", [128, 128], BF16)
        self.ecorr_bc = self.sb(c, "ecorr_bc", [128, 32], F32)
        self.ecorr_bf = self.sb(c, "ecorr_bf", [1, 32], BF16)
        self.bo_sb = self.sb(c, "bo_sb", [128, NIC], F32)
        self.gw_sb = self.sb(c, "gw_sb", [128, NIC, 32], BF16)
        self.combineT = self.sb(c, "combineT", [32, T], BF16)
        self.sel_sb = self.sb(c, "sel_sb", [32, NE * 128], BF16)
        self.eps_sb = self.sb(c, "eps_sb", [128, 1], F32)
        self.negE_sb = self.sb(c, "negE_sb", [128, 1], F32)
        for s in ["s_init", "s_gw", "s_bo", "s_xld", "s_xtf", "s_lnv", "s_sq",
                  "s_tmm", "s_tev", "s_scrw", "s_lmm", "s_sig", "s_ec", "s_vrt",
                  "s_cmm", "s_cev",
                  "s_w1", "s_kmm", "s_kev", "s_vmm", "s_vev", "s_vone",
                  "s_xq", "s_qmm", "s_qev", "s_smm", "s_ee", "s_pm", "s_pvm",
                  "s_bb", "s_on", "s_rbc", "s_rbe", "s_ost",
                  "s_w2", "s_omm", "s_oev", "s_otl", "s_xnh", "s_ce", "s_cev2",
                  "s_gmm", "s_umm", "s_dmm", "s_hsc", "s_acc", "s_xt2", "s_trm",
                  "s_oadd", "s_ostr", "s_sil", "s_vsc", "s_xld2", "s_xq2", "s_ost2", "s_dio", "s_wv", "s_wq", "s_gu0", "s_gu1", "s_dn0", "s_dn1", "s_gue", "s_dme", "s_wo", "s_oti", "s_oti2", "s_xnl", "s_acc0", "s_wom", "s_cem", "s_xt3", "s_xt4", "s_out", "s_out2", "s_xt5", "s_xt6", "s_out3", "s_out4"]:
            setattr(self, s, self.sem(s))

        self.blk = c.enter_context(nc.Block())
        ctx0 = ExitStack()
        self.ctx0_ps = ExitStack()
        self.emit_init()
        self.emit_p0(ctx0)
        if self.stop_after == "p0":
            self.emit_p0_finish(ctx0)
            self.ctx0_ps.close()
            ctx0.close()
            self.ctx.close()
            return nc
        ctx1 = ExitStack()
        self.ctx0_ps.close()
        self.emit_p1a(ctx0, ctx1)
        ctx0.close()
        self.emit_p1b(ctx1)
        if self.stop_after == "p1":
            self.emit_p1_finish()
            ctx1.close()
            self.ctx.close()
            return nc
        self.ctx_ps_p1b.close()
        ctx1.close()
        ctx2 = ExitStack()
        self.emit_p2(ctx2)
        self.ctx_ps_p2.close()
        ctx2.close()
        self.ctx.close()
        return nc

    def emit_init(self):
        blk = self.blk
        iota_t = self.sb(self.ctx, "iota_t", [128, 128], F32)
        self.iota_t = iota_t

        @blk.gpsimd
        def _(g):
            g.iota(iota_t[:], [[1, 128]], base=0, channel_multiplier=-1,
                   allow_small_or_imprecise_dtypes=True).then_inc(self.s_init, 1)
            g.dma_start(out=self.ecorr_bf[:], in_=self.ecorr[:, :]).then_inc(self.s_gw, 16)
            for ic in range(NIC):
                g.dma_start(out=self.gw_sb.ap()[:, ic, :],
                            in_=self.gwT[ic * 128:(ic + 1) * 128, :]).then_inc(self.s_gw, 16)

        @blk.sync
        def _(s):
            s.dma_start(out=self.sel_sb[:], in_=self.sel_in[:, :]).then_inc(self.s_bo, 16)
            for fc in range(NIC):
                s.dma_start(out=self.bo_sb.ap()[:, fc:fc + 1],
                            in_=self.bo[fc * 128:(fc + 1) * 128, :]).then_inc(self.s_bo, 16)

        @blk.vector
        def _(v):
            v.wait_ge(self.s_init, 1)
            v.tensor_single_scalar(self.ident_f[:], iota_t[:], 0.0, ALU.is_equal
                                   ).then_inc(self.s_init, 1)
            v.wait_ge(self.s_init, 2)
            v.tensor_copy(self.ident_bf[:], self.ident_f[:]).then_inc(self.s_init, 1)
            v.memset(self.ones_r[:], 1.0).then_inc(self.s_init, 1)
            v.memset(self.eps_sb[:], 1e-5).then_inc(self.s_init, 1)
            v.memset(self.negE_sb[:], -4.0).then_inc(self.s_init, 1)
        self.INIT_DONE = 6

    def emit_p0(self, c0):
        blk = self.blk
        self.XnT = self.sb(c0, "XnT", [128, NIC, T], BF16, side="right")
        xt = [self.sb(c0, f"xt{i}", [128, DM], F32, side="right") for i in range(2)]
        xm = self.sb(c0, "xm", [128, DM], F32, side="right")
        sq = self.sb(c0, "sq", [128, DM], F32, side="right")
        s1 = self.sb(c0, "s1", [128, 1], F32, side="right")
        mean = self.sb(c0, "mean", [128, 1], F32, side="right")
        ssum = self.sb(c0, "ssum", [128, 1], F32, side="right")
        sd = self.sb(c0, "sd", [128, 1], F32, side="right")
        rstd = self.sb(c0, "rstd", [128, 1], F32, side="right")
        xn_tm = self.sb(c0, "xn_tm", [128, DM], BF16, side="right")
        tp_ps = [self.ps(self.ctx0_ps, f"tp_ps{i}", [128, 128], BF16) for i in range(2)]
        lg_ps = [self.ps(self.ctx0_ps, f"lg_ps{i}", [128, 32]) for i in range(2)]
        ct_ps = [self.ps(self.ctx0_ps, f"ct_ps{i}", [32, 128], BF16) for i in range(2)]
        ec_ps = self.ps(self.ctx0_ps, "ec_ps", [128, 32])
        scores = self.sb(c0, "scores", [128, 32], F32, side="right")
        sfc = self.sb(c0, "sfc", [128, 32], F32, side="right")
        gs = self.sb(c0, "gs", [128, 8], F32, side="right")
        gt8 = self.sb(c0, "gt8", [128, 8], F32, side="right")
        gzap = self.sb(c0, "gzap", [128, 8], F32, side="right")
        emask = self.sb(c0, "emask", [128, 32], F32, side="right")
        masked = self.sb(c0, "masked", [128, 32], F32, side="right")
        t8 = self.sb(c0, "t8", [128, 8], F32, side="right")
        zap = self.sb(c0, "zap", [128, 32], F32, side="right")
        selm = self.sb(c0, "selm", [128, 32], F32, side="right")
        comb = self.sb(c0, "comb", [128, 32], F32, side="right")
        tsum = self.sb(c0, "tsum", [128, 1], F32, side="right")
        rcpt = self.sb(c0, "rcpt", [128, 1], F32, side="right")
        comb_bf = self.sb(c0, "comb_bf", [128, 32], BF16, side="right")

        @blk.sync
        def _(s):
            for tc in range(NTC):
                if tc >= 2:
                    s.wait_ge(self.s_xtf, tc - 1)
                s.dma_start(out=xt[tc % 2][:], in_=self.x[tc * 128:(tc + 1) * 128, :]
                            ).then_inc(self.s_xld if tc % 2 == 0 else self.s_xld2, 16)

        @blk.vector
        def _(v):
            for tc in range(NTC):
                x_ = xt[tc % 2][:]
                v.wait_ge(self.s_xld if tc % 2 == 0 else self.s_xld2, 16 * (tc // 2 + 1))
                self.vedge(v, v.reduce_sum(s1[:], x_, axis=AX.X))
                self.vedge(v, v.tensor_scalar_mul(mean[:], s1[:], 1.0 / DM))
                if tc >= 1:
                    v.wait_ge(self.s_sq, 2 * tc - 1)
                v.tensor_tensor(xm[:], x_, mean.ap().to_broadcast([128, DM]),
                                ALU.subtract).then_inc(self.s_xtf, 1)
                v.wait_ge(self.s_sq, 2 * tc + 1)
                v.reduce_sum(ssum[:], sq[:], axis=AX.X).then_inc(self.s_lnv, 1)
                v.wait_ge(self.s_sq, 2 * tc + 2)
                self.vedge(v, v.reciprocal(rstd[:], sd[:]))
                if tc >= 1:
                    v.wait_ge(self.s_tmm, 8 * tc)
                v.tensor_tensor(xn_tm[:], xm[:], rstd.ap().to_broadcast([128, DM]),
                                ALU.mult).then_inc(self.s_lnv, 1)

        @blk.scalar
        def _(s):
            for tc in range(NTC):
                s.wait_ge(self.s_xtf, tc + 1)
                s.activation(sq[:], xm[:], AF.Square).then_inc(self.s_sq, 1)
                s.wait_ge(self.s_lnv, 2 * tc + 1)
                if tc == 0:
                    s.wait_ge(self.s_init, 5)
                s.activation(sd[:], ssum[:], AF.Sqrt, bias=self.eps_sb[:, :], scale=1.0 / DM
                             ).then_inc(self.s_sq, 1)
                for fc in range(NIC):
                    n = 8 * tc + fc
                    s.wait_ge(self.s_tmm, n + 1)
                    s.copy(self.XnT.ap()[:, fc, tc * 128:(tc + 1) * 128],
                           tp_ps[n % 2][:, :]).then_inc(self.s_tev, 1)

        @blk.tensor
        def _(t):
            t.wait_ge(self.s_init, self.INIT_DONE)
            for tc in range(NTC):
                t.wait_ge(self.s_lnv, 2 * tc + 2)
                for fc in range(NIC):
                    n = 8 * tc + fc
                    if n >= 2:
                        t.wait_ge(self.s_tev, n - 1)
                    t.transpose(tp_ps[n % 2][:, :],
                                xn_tm.ap()[:, fc * 128:(fc + 1) * 128],
                                self.ident_bf[:]).then_inc(self.s_tmm, 1)

        @blk.gpsimd
        def _(g):
            for qt in range(NQT):
                g.wait_ge(self.s_tev, 8 * 4 * (qt + 1))
                for ic in range(NIC):
                    g.dma_start(out=self.xn_scr[qt, ic, :, :],
                                in_=self.XnT.ap()[:, ic, qt * 512:(qt + 1) * 512]
                                ).then_inc(self.s_scrw, 16)

        @blk.tensor
        def _(t):
            t.wait_ge(self.s_gw, 16 * 9)
            t.matmul(ec_ps[:, :], self.ones_r.ap()[0:1, :], self.ecorr_bf[:],
                     start=True, stop=True).then_inc(self.s_ec, 1)
            for tc in range(NTC):
                t.wait_ge(self.s_tev, 8 * (tc + 1))
                if tc >= 2:
                    t.wait_ge(self.s_sig, tc - 1)
                for ic in range(NIC):
                    i = t.matmul(lg_ps[tc % 2][:, :],
                                 self.XnT.ap()[:, ic, tc * 128:(tc + 1) * 128],
                                 self.gw_sb.ap()[:, ic, :], start=(ic == 0), stop=(ic == 7))
                i.then_inc(self.s_lmm, 1)
                t.wait_ge(self.s_vrt, 3 * tc + 3)
                if tc >= 2:
                    t.wait_ge(self.s_cev, tc - 1)
                t.transpose(ct_ps[tc % 2][:, :], comb_bf[:], self.ident_bf[:]
                            ).then_inc(self.s_cmm, 1)

        @blk.scalar
        def _(s):
            s.wait_ge(self.s_ec, 1)
            s.copy(self.ecorr_bc[:], ec_ps[:, :]).then_inc(self.s_ec, 1)
            for tc in range(NTC):
                s.wait_ge(self.s_lmm, tc + 1)
                if tc >= 1:
                    s.wait_ge(self.s_vrt, 3 * tc)
                s.activation(scores[:], lg_ps[tc % 2][:, :], AF.Sigmoid
                             ).then_inc(self.s_sig, 1)
                s.wait_ge(self.s_cmm, tc + 1)
                s.copy(self.combineT.ap()[:, tc * 128:(tc + 1) * 128],
                       ct_ps[tc % 2][:, :]).then_inc(self.s_cev, 1)

        @blk.vector
        def _(v):
            v.wait_ge(self.s_ec, 2)
            for tc in range(NTC):
                v.wait_ge(self.s_sig, tc + 1)
                E = lambda i: self.vedge(v, i)
                E(v.tensor_add(sfc[:], scores[:], self.ecorr_bc[:]))
                E(v.reduce_max(gs[:], sfc.ap().rearrange("p (g e) -> p g e", e=4),
                               axis=AX.X))
                E(v.max(out=gt8[:], in_=gs[:]))
                v.memset(gt8[:, 4:8], NEG).then_inc(self.s_vrt, 1)
                v.wait_ge(self.s_vrt, 3 * tc + 1)
                E(v.match_replace(out=gzap[:], in_to_replace=gt8[:], in_values=gs[:],
                                  imm_value=NEG))
                E(v.tensor_tensor(gs[:], gzap[:], gs[:], ALU.is_lt))
                for cc in range(4):
                    E(v.tensor_copy(emask.ap()[:, cc:32:4], gs.ap()[:, 0:8]))
                E(v.tensor_mul(masked[:], sfc[:], emask[:]))
                E(v.max(out=t8[:], in_=masked[:]))
                v.memset(t8[:, 6:8], NEG).then_inc(self.s_vrt, 1)
                v.wait_ge(self.s_vrt, 3 * tc + 2)
                E(v.match_replace(out=zap[:], in_to_replace=t8[:], in_values=masked[:],
                                  imm_value=NEG))
                E(v.tensor_tensor(selm[:], zap[:], masked[:], ALU.is_lt))
                E(v.tensor_mul(comb[:], scores[:], selm[:]))
                E(v.reduce_sum(tsum[:], comb[:], axis=AX.X))
                E(v.tensor_scalar_add(rcpt[:], tsum[:], 1e-20))
                E(v.reciprocal(rcpt[:], rcpt[:]))
                E(v.tensor_mul(comb[:], comb[:], rcpt.ap().to_broadcast([128, 32])))
                E(v.tensor_scalar_mul(comb[:], comb[:], 2.5))
                if tc >= 1:
                    v.wait_ge(self.s_cmm, tc)
                v.tensor_copy(comb_bf[:], comb[:]).then_inc(self.s_vrt, 1)

    def emit_p0_finish(self, c0):
        blk = self.blk
        cdbg = self.sb(c0, "cdbg", [32, T], F32)

        @blk.vector
        def _(v):
            v.wait_ge(self.s_cev, NTC)
            v.tensor_copy(cdbg[:], self.combineT[:]).then_inc(self.s_vrt, 1000)

        @blk.sync
        def _(s):
            s.wait_ge(self.s_vrt, 96 + 1000)
            s.dma_start(out=self.comb_dbg[:, :], in_=cdbg[:]).then_inc(self.s_xld, 16)
            s.wait_ge(self.s_xld, 16 * NTC + 16)
            s.wait_ge(self.s_scrw, 16 * NQT * NIC)


    def emit_p1a(self, c0, c1):
        blk = self.blk
        nc = self.nc
        self.c1 = c1
        self.wq_sb = self.sb(c1, "wq_sb", [128, NIC, DM], BF16)
        self.K_all = self.sb(c1, "K_all", [128, 4, T], BF16)
        self.vg = self.sb(c1, "vg", [128, NTC, NKV, 65], BF16)
        self.D = self.sb(c1, "D", [128, DW], F16)
        ctx_w = ExitStack()
        wk_sb = self.sb(ctx_w, "wkv_sb", [128, NIC, 512], BF16)
        wv_sb = wk_sb
        self.ctx_w_p1a = ctx_w
        ctx_ps = ExitStack()
        kp = [self.ps(ctx_ps, f"kp{i}", [128, 512]) for i in range(2)]
        vp = [self.ps(ctx_ps, f"vp{i}", [128, 512]) for i in range(2)]

        @blk.gpsimd
        def _(g):
            g.iota(self.D[:], [[1, DW]], base=-OFF, channel_multiplier=-1,
                   allow_small_or_imprecise_dtypes=True).then_inc(self.s_dio, 1)
            for ic in range(NIC):
                g.dma_start(out=wk_sb.ap()[:, ic, :],
                            in_=self.wkT[ic * 128:(ic + 1) * 128, :]).then_inc(self.s_w1, 16)
            g.wait_ge(self.s_kmm, 32)
            for ic in range(NIC):
                g.dma_start(out=wv_sb.ap()[:, ic, :],
                            in_=self.wvT[ic * 128:(ic + 1) * 128, :]).then_inc(self.s_wv, 16)
            for ic in range(NIC):
                g.dma_start(out=self.wq_sb.ap()[:, ic, :],
                            in_=self.wqT[ic * 128:(ic + 1) * 128, :]).then_inc(self.s_wq, 16)

        @blk.vector
        def _(v):
            v.memset(self.vg.ap()[:, :, :, 64:65], 1.0).then_inc(self.s_vone, 1)

        @blk.tensor
        def _(t):
            t.wait_ge(self.s_cev, NTC)
            t.wait_ge(self.s_tev, NTC * NIC)
            t.wait_ge(self.s_w1, 16 * 8)
            n = 0
            for kvc in range(4):
                for qt in range(NQT):
                    n += 1
                    if n > 2:
                        t.wait_ge(self.s_kev, n - 2)
                    for ic in range(NIC):
                        i = t.matmul(kp[n % 2][:, :],
                                     wk_sb.ap()[:, ic, kvc * 128:(kvc + 1) * 128],
                                     self.XnT.ap()[:, ic, qt * 512:(qt + 1) * 512],
                                     start=(ic == 0), stop=(ic == 7))
                    i.then_inc(self.s_kmm, 1)
            t.wait_ge(self.s_wv, 16 * 8)
            n = 0
            for tc in range(NTC):
                n += 1
                if n > 2:
                    t.wait_ge(self.s_vev, n - 2)
                for ic in range(NIC):
                    i = t.matmul(vp[n % 2][:, :],
                                 self.XnT.ap()[:, ic, tc * 128:(tc + 1) * 128],
                                 wv_sb.ap()[:, ic, :],
                                 start=(ic == 0), stop=(ic == 7))
                i.then_inc(self.s_vmm, 1)

        @blk.scalar
        def _(s):
            s.wait_ge(self.s_dio, 1)
            s.activation(self.D[:], self.D[:], AF.Abs)
            for n in range(1, 33):
                kvc, qt = (n - 1) // NQT, (n - 1) % NQT
                s.wait_ge(self.s_kmm, n)
                s.copy(self.K_all.ap()[:, kvc, qt * 512:(qt + 1) * 512],
                       kp[n % 2][:, :]).then_inc(self.s_kev, 1)

        @blk.vector
        def _(v):
            for n in range(1, 33):
                tc = n - 1
                v.wait_ge(self.s_vmm, n)
                v.tensor_copy(self.vg.ap()[:, tc, :, 0:64],
                              vp[n % 2].ap().rearrange("p (j d) -> p j d", d=64)
                              ).then_inc(self.s_vev, 1)

        self.ctx_ps_p1a = ctx_ps
        ctx_ps.close()
        ctx_w.close()

    def emit_p1b(self, c1):
        blk = self.blk
        nc = self.nc
        Q_sb = [self.sb(c1, f"Q_sb{i}", [128, T], BF16) for i in range(2)]
        xq_sb = [self.sb(c1, f"xq_sb{i}", [128, NIC, 512], BF16) for i in range(2)]
        Bp = self.sb(c1, "Bp", [128, 2, DW], BF16)
        e_sb = [self.sb(c1, f"e_sb{i}", [128, QT1], BF16) for i in range(2)]
        p_sb = [self.sb(c1, f"p_sb{i}", [128, QT1], BF16) for i in range(2)]
        o_nm = [self.sb(c1, f"o_nm{i}", [128, 512], BF16) for i in range(2)]
        rcp_f = self.sb(c1, "rcp_f", [128, QT1], F32)
        rcpb = self.sb(c1, "rcpb", [128, QT1], BF16)
        rbc_sb = self.sb(c1, "rbc_sb", [64, QT1], F32)
        ctx_ps = ExitStack()
        st_ps = [self.ps(ctx_ps, f"st_ps{i}", [128, QT1]) for i in range(2)]
        o2_ps = self.ps(ctx_ps, "o2_ps", [65, QT1])
        rbc_ps = self.ps(ctx_ps, "rbc_ps", [64, 512])
        qp_ps = self.ps(ctx_ps, "qp_ps", [128, 512])
        self.ctx_ps_p1b = ctx_ps

        SLOPES = [2.0 ** (-(h + 1)) for h in range(NH)]

        @blk.sync
        def _(s):
            s.wait_ge(self.s_kev, 32)
            s.wait_ge(self.s_vev, 32)
            for jp in range(NKV):
                for qt in range(NQT):
                    nq = NQT * jp + qt
                    s.wait_ge(self.s_scrw, 16 * NIC * NQT)
                    if nq >= 2:
                        s.wait_ge(self.s_qmm, nq - 1)
                    s.dma_start(out=xq_sb[nq % 2][:],
                                in_=self.xn_scr[qt].transpose([1, 0, 2])
                                ).then_inc(self.s_xq if nq % 2 == 0 else self.s_xq2, 16)

        @blk.gpsimd
        def _(g):
            for jp in range(NKV):
                hA, hB = HEAD_PAIRS[jp]
                for q5 in range(8):
                    m = 8 * jp + q5 + 1
                    qd, co = q5 // 2, (q5 % 2) * 512
                    g.wait_ge(self.s_on, 2 * m)
                    par = self.s_ost if (m - 1) % 2 == 0 else self.s_ost2
                    g.dma_start(out=self.o_scr[qd, hA // 2, (hA % 2) * 64:
                                               (hA % 2) * 64 + 64, co:co + 512],
                                in_=o_nm[(m - 1) % 2].ap()[0:64, :]).then_inc(par, 16)
                    g.dma_start(out=self.o_scr[qd, hB // 2, (hB % 2) * 64:
                                               (hB % 2) * 64 + 64, co:co + 512],
                                in_=o_nm[(m - 1) % 2].ap()[64:128, :]).then_inc(par, 16)

        self._pending_rbc = None

        @blk.tensor
        def _(t):
            t.wait_ge(self.s_kev, 32)
            t.wait_ge(self.s_vev, 32)
            t.wait_ge(self.s_wq, 16 * 8)
            t.wait_ge(self.s_vone, 1)
            step = 0
            for jp in range(NKV):
                if self._pending_rbc:
                    mm_ = self._pending_rbc
                    t.wait_ge(self.s_rbc, 2 * mm_)
                    t.matmul(rbc_ps[:, :], self.ones_r.ap()[64:65, 0:64],
                             rcpb.ap()[64:65, 0:512], start=True, stop=True
                             ).then_inc(self.s_omm, 1)
                    t.wait_ge(self.s_rbe, 2 * mm_ - 1)
                    t.matmul(rbc_ps[:, :], self.ones_r.ap()[64:65, 0:64],
                             rcpb.ap()[64:65, 512:1024], start=True, stop=True
                             ).then_inc(self.s_omm, 1)
                    self._pending_rbc = None
                for qt in range(NQT):
                    nq = NQT * jp + qt
                    if qt == 0:
                        t.wait_ge(self.s_qev, NQT * jp)
                        for _ in range(10):
                            t.matmul(qp_ps.ap()[0:1, 0:64], self.ones_r.ap()[0:1, 0:1],
                                     self.ones_r.ap()[0:1, 0:64], start=True, stop=True)
                    t.wait_ge(self.s_xq if nq % 2 == 0 else self.s_xq2,
                              16 * (nq // 2 + 1))
                    if jp >= 2 and qt == 0:
                        t.wait_ge(self.s_smm, (jp - 1) * 256)
                    if nq >= 1:
                        t.wait_ge(self.s_qev, nq)
                    for ic in range(NIC):
                        i = t.matmul(qp_ps[:, :],
                                     self.wq_sb.ap()[:, ic, jp * 128:(jp + 1) * 128],
                                     xq_sb[nq % 2].ap()[:, ic, :],
                                     start=(ic == 0), stop=(ic == 7))
                    i.then_inc(self.s_qmm, 1)
                hA, hB = HEAD_PAIRS[jp]
                kvA, kvB = hA // 2, hB // 2
                for q5 in range(8):
                    m = 8 * jp + q5 + 1
                    q0 = q5 * 512
                    t.wait_ge(self.s_qev, NQT * (jp + 1))
                    if self._pending_rbc:
                        for _ in range(10):
                            t.matmul(qp_ps.ap()[0:1, 0:64], self.ones_r.ap()[0:1, 0:1],
                                     self.ones_r.ap()[0:1, 0:64], start=True, stop=True)
                    if self._pending_rbc:
                        mm_ = self._pending_rbc
                        t.wait_ge(self.s_rbc, 2 * mm_)
                        t.matmul(rbc_ps[:, :], self.ones_r.ap()[64:65, 0:64],
                                 rcpb.ap()[64:65, 0:512], start=True, stop=True
                                 ).then_inc(self.s_omm, 1)
                        t.wait_ge(self.s_rbe, 2 * mm_ - 1)
                        t.matmul(rbc_ps[:, :], self.ones_r.ap()[64:65, 0:64],
                                 rcpb.ap()[64:65, 512:1024], start=True, stop=True
                                 ).then_inc(self.s_omm, 1)
                        self._pending_rbc = None
                    def emit_Spair(kc, base=None):
                        ns = base + kc + 1
                        if ns > 2:
                            t.wait_ge(self.s_ee, ns - 2)
                        t.matmul(st_ps[ns % 2].ap()[:, 0:512],
                                 self.K_all.ap()[0:64, kvA // 2,
                                                 kc * 128:(kc + 1) * 128],
                                 Q_sb[jp % 2].ap()[0:64, q0:q0 + 512],
                                 start=True, stop=True)
                        i = t.matmul(st_ps[ns % 2].ap()[:, 512:1024],
                                     self.K_all.ap()[64:128, kvB // 2,
                                                     kc * 128:(kc + 1) * 128],
                                     Q_sb[jp % 2].ap()[64:128, q0:q0 + 512],
                                     start=True, stop=True)
                        i.then_inc(self.s_smm, 1)

                    base = step
                    emit_Spair(0, base)
                    emit_Spair(1, base)
                    for kc in range(NKC):
                        step += 1
                        if kc + 2 < NKC:
                            emit_Spair(kc + 2, base)
                        t.matmul(qp_ps.ap()[0:1, 0:64], self.ones_r.ap()[0:1, 0:1],
                                 self.ones_r.ap()[0:1, 0:64], start=True, stop=True)
                        t.wait_ge(self.s_pm, step)
                        if kc == 0:
                            t.wait_ge(self.s_on, 2 * (m - 1))
                        t.matmul(o2_ps.ap()[:, 0:512], self.vg.ap()[:, kc, kvA, :],
                                 p_sb[step % 2].ap()[:, 0:512],
                                 start=(kc == 0), stop=(kc == NKC - 1))
                        i = t.matmul(o2_ps.ap()[:, 512:1024], self.vg.ap()[:, kc, kvB, :],
                                     p_sb[step % 2].ap()[:, 512:1024],
                                     start=(kc == 0), stop=(kc == NKC - 1))
                        i.then_inc(self.s_pvm, 1)
                    self._pending_rbc = m
            if self._pending_rbc:
                mm_ = self._pending_rbc
                t.wait_ge(self.s_rbc, 2 * mm_)
                t.matmul(rbc_ps[:, :], self.ones_r.ap()[64:65, 0:64],
                         rcpb.ap()[64:65, 0:512], start=True, stop=True
                         ).then_inc(self.s_omm, 1)
                t.wait_ge(self.s_rbe, 2 * mm_ - 1)
                t.matmul(rbc_ps[:, :], self.ones_r.ap()[64:65, 0:64],
                         rcpb.ap()[64:65, 512:1024], start=True, stop=True
                         ).then_inc(self.s_omm, 1)
                self._pending_rbc = None

        @blk.scalar
        def _(s):
            s.wait_ge(self.s_init, 6)
            step = 0
            for jp in range(NKV):
                for qt in range(NQT):
                    nq = NQT * jp + qt
                    s.wait_ge(self.s_qmm, nq + 1)
                    s.copy(Q_sb[jp % 2].ap()[:, qt * 512:(qt + 1) * 512],
                           qp_ps[:, :]).then_inc(self.s_qev, 1)
                hA, hB = HEAD_PAIRS[jp]
                if jp >= 1:
                    s.wait_ge(self.s_pm, jp * 256)
                for bi, hh in ((0, hA), (1, hB)):
                    for seg in range(8):
                        s.activation(Bp.ap()[:, bi, seg * 1024:(seg + 1) * 1024],
                                     self.D.ap()[:, seg * 1024:(seg + 1) * 1024],
                                     AF.Exp, scale=-SLOPES[hh] / 8.0)
                s.nop().then_inc(self.s_bb, 1)
                for q5 in range(8):
                    m = 8 * jp + q5 + 1
                    for kc in range(NKC):
                        step += 1
                        s.wait_ge(self.s_smm, step)
                        if step > 2:
                            s.wait_ge(self.s_pm, step - 2)
                        s.activation(e_sb[step % 2][:], st_ps[step % 2][:, :],
                                     AF.Exp, bias=self.negE_sb[:, :], scale=0.125
                                     ).then_inc(self.s_ee, 1)
                    s.wait_ge(self.s_pvm, step)
                    if m >= 2:
                        s.wait_ge(self.s_omm, 2 * (m - 1))
                    self.sedge(s, s.activation(rcp_f.ap()[64:65, :],
                                               o2_ps.ap()[64:65, :], AF.Ln))
                    s.activation(rcpb.ap()[64:65, :], rcp_f.ap()[64:65, :],
                                 AF.Exp, scale=-1.0).then_inc(self.s_rbc, 2)
                    s.wait_ge(self.s_omm, 2 * m - 1)
                    s.copy(rbc_sb.ap()[:, 0:512], rbc_ps[:, :]).then_inc(self.s_rbe, 1)
                    s.wait_ge(self.s_omm, 2 * m)
                    s.copy(rbc_sb.ap()[:, 512:1024], rbc_ps[:, :]).then_inc(self.s_rbe, 1)

        @blk.vector
        def _(v):
            step = 0
            for jp in range(NKV):
                for q5 in range(8):
                    m = 8 * jp + q5 + 1
                    q0 = q5 * 512
                    for kc in range(NKC):
                        step += 1
                        v.wait_ge(self.s_ee, step)
                        v.wait_ge(self.s_bb, jp + 1)
                        if step > 2:
                            v.wait_ge(self.s_pvm, step - 2)
                        off = q0 - kc * 128 + OFF
                        v.tensor_mul(p_sb[step % 2].ap().rearrange(
                                         "p (b q) -> p b q", b=2),
                                     e_sb[step % 2].ap().rearrange(
                                         "p (b q) -> p b q", b=2),
                                     Bp.ap()[:, :, off:off + 512]
                                     ).then_inc(self.s_pm, 1)
                    v.wait_ge(self.s_rbe, 2 * m)
                    if m >= 3:
                        v.wait_ge(self.s_ost if (m - 1) % 2 == 0 else self.s_ost2,
                                  32 * ((m - 1) // 2))
                    v.tensor_mul(o_nm[(m - 1) % 2].ap()[0:64, :],
                                 o2_ps.ap()[0:64, 0:512], rbc_sb.ap()[:, 0:512])
                    v.tensor_mul(o_nm[(m - 1) % 2].ap()[64:128, :],
                                 o2_ps.ap()[0:64, 512:1024], rbc_sb.ap()[:, 512:1024]
                                 ).then_inc(self.s_on, 2)

    def emit_p1_finish(self):
        blk = self.blk

        @blk.sync
        def _(s):
            s.wait_ge(self.s_ost, 16 * 64)
            s.wait_ge(self.s_ost2, 16 * 64)
            s.wait_ge(self.s_scrw, 16 * NQT * NIC)
            s.wait_ge(self.s_bo, 16 * 9)
        self.ctx_ps_p1b.close()

    def emit_p2(self, c2):
        blk = self.blk
        NEQ = NE * NQ1
        acc = self.sb(c2, "acc", [128, NIC, QT1], F32)
        xnh = self.sb(c2, "xnh", [128, NIC, QT1], BF16)
        gu_sb = [self.sb(c2, f"gu_sb{i}", [128, NIC, 1024], BF16) for i in range(2)]
        dn_sb = [self.sb(c2, f"dn_sb{i}", [128, 4, 1024], BF16) for i in range(2)]
        h_sb = self.sb(c2, "h_sb", [128, 4, QT1], BF16)
        silu_sb = self.sb(c2, "silu_sb", [128, QT1], BF16)
        ce_bc = [self.sb(c2, f"ce_bc{i}", [128, QT1], BF16) for i in range(2)]
        ot_sb = self.sb(c2, "ot_sb", [128, NIC, 512], BF16)
        wo_sb = self.sb(c2, "wo_sb", [128, NIC, DM], BF16)
        x_t = [self.sb(c2, f"x_t{i}", [128, DM], F32) for i in range(4)]
        out_sb = [self.sb(c2, f"out_sb{i}", [128, DM], F32) for i in range(4)]
        ctx_ps = ExitStack()
        g_ps = self.ps(ctx_ps, "g_ps", [128, QT1])
        u_ps = self.ps(ctx_ps, "u_ps", [128, QT1])
        dn_ps = [self.ps(ctx_ps, f"dn_ps{i}", [128, 512]) for i in range(2)]
        wo_ps = [self.ps(ctx_ps, f"wo_ps{i}", [128, 512]) for i in range(2)]
        self.ctx_ps_p2 = ctx_ps
        s_gu = [self.s_gu0, self.s_gu1]
        s_dn = [self.s_dn0, self.s_dn1]
        P1_FENCE = [(self.s_on, 2 * 64), (self.s_ost, 16 * 64),
                    (self.s_ost2, 16 * 64), (self.s_pvm, 2048)]

        @blk.gpsimd
        def _(g):
            for sem, val in P1_FENCE:
                g.wait_ge(sem, val)
            g.dma_start(out=wo_sb[:],
                        in_=self.woT.rearrange("(ic p) c -> p ic c", p=128)
                        ).then_inc(self.s_wo, 16)
            for gg in range(NEQ):
                e = gg % NE
                if gg >= 2:
                    g.wait_ge(self.s_gue, gg - 1)
                g.dma_start(out=gu_sb[gg % 2][:],
                            in_=self.guT[e].rearrange("(ic p) c -> p ic c", p=128)
                            ).then_inc(s_gu[gg % 2], 16)
                if gg >= 2:
                    g.wait_ge(self.s_dme, gg - 1)
                g.dma_start(out=dn_sb[gg % 2][:],
                            in_=self.dnT[e].rearrange("(cc p) c -> p cc c", p=128)
                            ).then_inc(s_dn[gg % 2], 16)

        @blk.sync
        def _(s):
            for sem, val in P1_FENCE:
                s.wait_ge(sem, val)
            s_xtq = [self.s_xt3, self.s_xt4, self.s_xt5, self.s_xt6]
            s_outq = [self.s_out, self.s_out2, self.s_out3, self.s_out4]
            for qd in range(NQ1):
                if qd >= 1:
                    for sq in s_outq:
                        s.wait_ge(sq, 16 * 2 * qd)
                for half in range(2):
                    s.dma_start(out=xnh.ap()[:, :, half * 512:(half + 1) * 512],
                                in_=self.xn_scr[2 * qd + half].transpose([1, 0, 2])
                                ).then_inc(self.s_xnl, 16)
                for half in range(2):
                    if half == 1:
                        s.wait_ge(self.s_wom, 8 * (2 * qd + 1))
                    s.dma_start(out=ot_sb[:],
                                in_=self.o_scr[qd, :, :, half * 512:(half + 1) * 512]
                                .transpose([1, 0, 2])
                                ).then_inc(self.s_oti if half == 0 else self.s_oti2, 16)
                for tc in range(8):
                    nt = 8 * qd + tc
                    if nt >= 4:
                        s.wait_ge(self.s_oadd, 2 * nt - 6)
                    s.dma_start(out=x_t[nt % 4][:],
                                in_=self.x[qd * QT1 + tc * 128:
                                           qd * QT1 + (tc + 1) * 128, :]
                                ).then_inc(s_xtq[nt % 4], 16)
                    s.wait_ge(self.s_oadd, 2 * nt + 2)
                    s.dma_start(out=self.out[qd * QT1 + tc * 128:
                                             qd * QT1 + (tc + 1) * 128, :],
                                in_=out_sb[nt % 4][:]
                                ).then_inc(s_outq[nt % 4], 16)
            for sq in s_outq:
                s.wait_ge(sq, 16 * 8)

        @blk.tensor
        def _(t):
            t.wait_ge(self.s_wo, 16)
            t.wait_ge(self.s_bo, 16 * 9)
            nd = 0
            nh_ = 0
            nsil = 0
            for qd in range(NQ1):
                for half in range(2):
                    t.wait_ge(self.s_oti if half == 0 else self.s_oti2, 16 * (qd + 1))
                    if qd >= 1 and half == 0:
                        t.wait_ge(self.s_ce, 2 * NRE * qd)
                    for fc in range(NIC):
                        nw = 16 * qd + 8 * half + fc + 1
                        if nw > 2:
                            t.wait_ge(self.s_acc0, nw - 2)
                        for hc in range(NIC):
                            i = t.matmul(wo_ps[nw % 2][:, :],
                                         wo_sb.ap()[:, hc, fc * 128:(fc + 1) * 128],
                                         ot_sb.ap()[:, hc, :],
                                         start=(hc == 0), stop=(hc == 7))
                        i.then_inc(self.s_wom, 1)
                for e in range(NE):
                    gg = qd * NE + e
                    routed = e < NRE
                    t.wait_ge(s_gu[gg % 2], 16 * (gg // 2 + 1))
                    t.wait_ge(self.s_xnl, 16 * 2 * (qd + 1))
                    if routed:
                        if e == 0:
                            t.wait_ge(self.s_acc0, 16 * (qd + 1))
                        for half in range(2):
                            t.wait_ge(self.s_hsc, max(0, nh_ - 6))
                            i = t.matmul(wo_ps[half][:, :],
                                         self.sel_sb.ap()[:, e * 128:(e + 1) * 128],
                                         self.combineT.ap()[:, qd * QT1 + half * 512:
                                                            qd * QT1 + (half + 1) * 512],
                                         start=True, stop=True)
                            i.then_inc(self.s_cmm, 1)
                    for cc in range(4):
                        for op, ps, base in (("g", g_ps, 0), ("u", u_ps, 512)):
                            for tt in range(2):
                                if op == "g":
                                    t.wait_ge(self.s_sil, max(0, nsil - 1))
                                else:
                                    t.wait_ge(self.s_hsc, max(0, nh_ - 1))
                                for ic in range(NIC):
                                    i = t.matmul(ps.ap()[:, tt * 512:(tt + 1) * 512],
                                                 gu_sb[gg % 2].ap()[:, ic,
                                                                    base + cc * 128:
                                                                    base + (cc + 1) * 128],
                                                 xnh.ap()[:, ic, tt * 512:(tt + 1) * 512],
                                                 start=(ic == 0), stop=(ic == 7))
                                i.then_inc(self.s_gmm if op == "g" else self.s_umm, 1)
                                if op == "g":
                                    nsil += 1
                                else:
                                    nh_ += 1

                    t.wait_ge(self.s_umm, nh_)
                    t.nop().then_inc(self.s_gue, 1)
                    t.wait_ge(s_dn[gg % 2], 16 * (gg // 2 + 1))
                    for fc in range(NIC):
                        for tt in range(2):
                            nd += 1
                            if nd > 2:
                                t.wait_ge(self.s_acc, nd - 2)
                            t.wait_ge(self.s_hsc, nh_)
                            for cc in range(4):
                                i = t.matmul(dn_ps[nd % 2][:, :],
                                             dn_sb[gg % 2].ap()[:, cc, fc * 128:(fc + 1) * 128],
                                             h_sb.ap()[:, cc, tt * 512:(tt + 1) * 512],
                                             start=(cc == 0), stop=(cc == 3))
                            i.then_inc(self.s_dmm, 1)
                    t.wait_ge(self.s_dmm, nd)
                    t.nop().then_inc(self.s_dme, 1)
                t.wait_ge(self.s_acc, nd)
                for tc in range(8):
                    nt = 8 * qd + tc
                    for fc in range(NIC):
                        bank = fc // 4
                        ntr = 2 * nt + bank + 1
                        if ntr > 2:
                            t.wait_ge(self.s_oadd, ntr - 2)
                        i = t.transpose(dn_ps[bank].ap()[:, (fc % 4) * 128:
                                                         (fc % 4 + 1) * 128],
                                        acc.ap()[:, fc, tc * 128:(tc + 1) * 128],
                                        self.ident_f[:])
                        if fc % 4 == 3:
                            i.then_inc(self.s_trm, 1)

        @blk.scalar
        def _(s):
            nsil = 0
            for qd in range(NQ1):
                for half in range(2):
                    for fc in range(NIC):
                        nw = 16 * qd + 8 * half + fc + 1
                        s.wait_ge(self.s_wom, nw)
                        s.activation(acc.ap()[:, fc, half * 512:(half + 1) * 512],
                                     wo_ps[nw % 2][:, :], AF.Identity,
                                     bias=self.bo_sb.ap()[:, fc:fc + 1]
                                     ).then_inc(self.s_acc0, 1)
                for e in range(NE):
                    gg = qd * NE + e
                    routed = e < NRE
                    if routed:
                        for half in range(2):
                            s.wait_ge(self.s_cmm, NTC + 2 * (qd * NRE + e) + half + 1)
                            s.copy(ce_bc[gg % 2].ap()[:, half * 512:(half + 1) * 512],
                                   wo_ps[half][:, :]).then_inc(self.s_ce, 1)
                    for cc in range(4):
                        for tt in range(2):
                            nsil += 1
                            s.wait_ge(self.s_gmm, nsil)
                            if nsil > 2:
                                s.wait_ge(self.s_hsc, nsil - 2)
                            s.activation(silu_sb.ap()[:, tt * 512:(tt + 1) * 512],
                                         g_ps.ap()[:, tt * 512:(tt + 1) * 512],
                                         AF.Silu).then_inc(self.s_sil, 1)

        @blk.vector
        def _(v):
            vs_xtq = [self.s_xt3, self.s_xt4, self.s_xt5, self.s_xt6]
            vs_outq = [self.s_out, self.s_out2, self.s_out3, self.s_out4]
            nd = 0
            nh_ = 0
            for qd in range(NQ1):
                for e in range(NE):
                    gg = qd * NE + e
                    routed = e < NRE
                    for cc in range(4):
                        for tt in range(2):
                            nh_ += 1
                            sl = slice(tt * 512, (tt + 1) * 512)
                            if cc == 0 and tt == 0 and gg >= 1:
                                v.wait_ge(self.s_dmm, 16 * gg)
                            v.wait_ge(self.s_sil, nh_)
                            v.wait_ge(self.s_umm, nh_)
                            i = v.scalar_tensor_tensor(h_sb.ap()[:, cc, sl],
                                                       u_ps.ap()[:, sl], 1.0,
                                                       silu_sb.ap()[:, sl],
                                                       ALU.mult, ALU.mult)
                            if routed:
                                self.vedge(v, i)
                                if cc == 0 and tt == 0:
                                    v.wait_ge(self.s_ce, 2 * (qd * NRE + e + 1))
                                i = v.tensor_mul(h_sb.ap()[:, cc, sl],
                                                 h_sb.ap()[:, cc, sl],
                                                 ce_bc[gg % 2].ap()[:, sl])
                            i.then_inc(self.s_hsc, 1)
                    for fc in range(NIC):
                        for tt in range(2):
                            nd += 1
                            v.wait_ge(self.s_dmm, nd)
                            sl = slice(tt * 512, (tt + 1) * 512)
                            v.tensor_tensor(acc.ap()[:, fc, sl], dn_ps[nd % 2][:, :],
                                            acc.ap()[:, fc, sl], ALU.add
                                            ).then_inc(self.s_acc, 1)
                for tc in range(8):
                    nt = 8 * qd + tc
                    for bank in range(2):
                        ntr = 2 * nt + bank + 1
                        v.wait_ge(self.s_trm, ntr)
                        v.wait_ge(vs_xtq[nt % 4], 16 * (nt // 4 + 1))
                        if nt >= 4 and bank == 0:
                            v.wait_ge(vs_outq[nt % 4], 16 * (nt // 4))
                        v.tensor_add(out_sb[nt % 4].ap()[:, bank * 512:(bank + 1) * 512],
                                     dn_ps[bank][:, :],
                                     x_t[nt % 4].ap()[:, bank * 512:(bank + 1) * 512]
                                     ).then_inc(self.s_oadd, 1)

import ml_dtypes
from concourse.bass_utils import run_bass_kernel_spmd

_NC_CACHE = {}


def _build_program():
    if "nc" not in _NC_CACHE:
        _NC_CACHE["nc"] = MK(stop_after=None, debug=False).build()
    return _NC_CACHE["nc"]


def kernel(L0_attn_x, L0_attn_Wq, L0_attn_Wk, L0_attn_Wv, L0_attn_Wo, L0_attn_bo,
           L0_ffn_gate_weight, L0_ffn_e_score_correction_bias, L0_ffn_experts_gate_up,
           L0_ffn_experts_down, L0_ffn_shared_gate_w, L0_ffn_shared_up_w,
           L0_ffn_shared_down_w):
    x = np.asarray(L0_attn_x)
    wqT_full = np.asarray(L0_attn_Wq).T.reshape(DM, NH, HD)
    order = [h for pair in HEAD_PAIRS for h in pair]
    wqT = np.ascontiguousarray(wqT_full[:, order, :].reshape(DM, DM))
    wkT = np.ascontiguousarray(np.asarray(L0_attn_Wk).T)
    wvT = np.ascontiguousarray(np.asarray(L0_attn_Wv).T)
    woT = np.ascontiguousarray(np.asarray(L0_attn_Wo).T)
    bo = np.ascontiguousarray(np.asarray(L0_attn_bo).reshape(DM, 1))
    gwT = np.ascontiguousarray(np.asarray(L0_ffn_gate_weight).T)
    ecorr = np.ascontiguousarray(
        np.asarray(L0_ffn_e_score_correction_bias).reshape(1, 32))
    sgw = np.asarray(L0_ffn_shared_gate_w)
    suw = np.asarray(L0_ffn_shared_up_w)
    sdw = np.asarray(L0_ffn_shared_down_w)
    guT = np.concatenate([
        np.ascontiguousarray(np.swapaxes(np.asarray(L0_ffn_experts_gate_up), 1, 2)),
        np.stack([np.concatenate([sgw[512 * j:512 * (j + 1)].T,
                                  suw[512 * j:512 * (j + 1)].T], axis=1)
                  for j in range(2)]),
    ], axis=0).astype(np.float32)
    dnT = np.concatenate([
        np.ascontiguousarray(np.swapaxes(np.asarray(L0_ffn_experts_down), 1, 2)),
        np.stack([np.ascontiguousarray(sdw[:, 512 * j:512 * (j + 1)].T)
                  for j in range(2)]),
    ], axis=0).astype(np.float32)
    sel = np.zeros((32, NE * 128), dtype=ml_dtypes.bfloat16)
    for e in range(32):
        sel[e, 128 * e:128 * (e + 1)] = 1.0

    nc = _build_program()
    in_maps = []
    for core in range(8):
        in_maps.append({
            "x": np.ascontiguousarray(x[core]),
            "wqT": wqT, "wkT": wkT, "wvT": wvT, "woT": woT, "bo": bo,
            "gwT": gwT, "ecorr": ecorr, "sel_in": sel, "guT": guT, "dnT": dnT,
        })
    res = run_bass_kernel_spmd(nc, in_maps, core_ids=list(range(8)))
    out = np.stack([res.results[i]["out"] for i in range(8)], axis=0)
    return out


# version 14 (final)
# speedup vs baseline: 1.0089x; 1.0089x over previous
import numpy as np
from contextlib import ExitStack
import concourse.bass as bass
import concourse.mybir as mybir

F32 = mybir.dt.float32
BF16 = mybir.dt.bfloat16
F16 = mybir.dt.float16
AF = mybir.ActivationFunctionType
ALU = mybir.AluOpType
AX = mybir.AxisListType

T = 4096
DM = 1024
NIC = 8
NTC = 32
NQT = 8
NH = 16
HD = 64
NKV = 8
NE = 34
NRE = 32
EI = 512
OFF = 3968
DW = 8192
QT1 = 1024
NQ1 = T // QT1
NKC = T // 128
NEG = -1e30
ECONST = 4.0
HEAD_PAIRS = [(0, 2), (1, 3), (4, 6), (5, 7), (8, 10), (9, 11), (12, 14), (13, 15)]


class MK:
    def __init__(self, stop_after=None, debug=False):
        self.stop_after = stop_after
        self.debug = debug
        nc = bass.Bass("TRN2", target_bir_lowering=False)
        self.nc = nc
        d = nc.dram_tensor
        scr_kind = "ExternalOutput" if debug else "Internal"
        self.x = d("x", [T, DM], F32, kind="ExternalInput").ap()
        self.wqT = d("wqT", [DM, DM], F32, kind="ExternalInput").ap()
        self.wkT = d("wkT", [DM, 512], F32, kind="ExternalInput").ap()
        self.wvT = d("wvT", [DM, 512], F32, kind="ExternalInput").ap()
        self.woT = d("woT", [DM, DM], F32, kind="ExternalInput").ap()
        self.bo = d("bo", [DM, 1], F32, kind="ExternalInput").ap()
        self.gwT = d("gwT", [DM, 32], F32, kind="ExternalInput").ap()
        self.ecorr = d("ecorr", [1, 32], F32, kind="ExternalInput").ap()
        self.sel_in = d("sel_in", [32, NE * 128], BF16, kind="ExternalInput").ap()
        self.guT = d("guT", [NE, DM, 1024], F32, kind="ExternalInput").ap()
        self.dnT = d("dnT", [NE, EI, DM], F32, kind="ExternalInput").ap()
        self.out = d("out", [T, DM], F32, kind="ExternalOutput").ap()
        self.xn_scr = d("xn_scr", [NQT, NIC, 128, 512], BF16, kind=scr_kind).ap()
        self.o_scr = d("o_scr", [NQ1, NIC, 128, QT1], BF16, kind=scr_kind).ap()
        if debug:
            self.comb_dbg = d("comb_dbg", [32, T], F32, kind="ExternalOutput").ap()

        self.ctx = ExitStack()
        self.sems = {}
        self.n_vsc = 0
        self.n_sedge = 0

    def sb(self, ctx, name, shape, dtype, side="left"):
        return ctx.enter_context(self.nc.sbuf_tensor(name, shape, dtype, side=side))

    def ps(self, ctx, name, shape, dtype=F32):
        return ctx.enter_context(self.nc.psum_tensor(name, shape, dtype))

    def sedge(self, s, instr):
        self.n_sedge += 1
        instr.then_inc(self.s_ostr, 1)
        s.wait_ge(self.s_ostr, self.n_sedge)

    def vedge(self, v, instr):
        self.n_vsc += 1
        instr.then_inc(self.s_vsc, 1)
        v.wait_ge(self.s_vsc, self.n_vsc)

    def sem(self, name):
        s = self.ctx.enter_context(self.nc.semaphore(name))
        self.sems[name] = s
        return s

    def build(self):
        nc = self.nc
        c = self.ctx
        self.ident_bf = self.sb(c, "ident_bf", [128, 128], BF16)
        self.ident_f = self.sb(c, "ident_f", [128, 128], F32)
        self.ones_r = self.sb(c, "ones_r", [128, 128], BF16)
        self.ecorr_bc = self.sb(c, "ecorr_bc", [128, 32], F32)
        self.ecorr_bf = self.sb(c, "ecorr_bf", [1, 32], BF16)
        self.bo_sb = self.sb(c, "bo_sb", [128, NIC], F32)
        self.gw_sb = self.sb(c, "gw_sb", [128, NIC, 32], BF16)
        self.combineT = self.sb(c, "combineT", [32, T], BF16)
        self.sel_sb = self.sb(c, "sel_sb", [32, NE * 128], BF16)
        self.eps_sb = self.sb(c, "eps_sb", [128, 1], F32)
        self.negE_sb = self.sb(c, "negE_sb", [128, 1], F32)
        for s in ["s_init", "s_gw", "s_bo", "s_xld", "s_xtf", "s_lnv", "s_sq",
                  "s_tmm", "s_tev", "s_scrw", "s_lmm", "s_sig", "s_ec", "s_vrt",
                  "s_cmm", "s_cev",
                  "s_w1", "s_kmm", "s_kev", "s_vmm", "s_vev", "s_vone",
                  "s_xq", "s_qmm", "s_qev", "s_smm", "s_ee", "s_pm", "s_pvm",
                  "s_bb", "s_on", "s_rbc", "s_rbe", "s_ost",
                  "s_w2", "s_omm", "s_oev", "s_otl", "s_xnh", "s_ce", "s_cev2",
                  "s_gmm", "s_umm", "s_dmm", "s_hsc", "s_acc", "s_xt2", "s_trm",
                  "s_oadd", "s_ostr", "s_sil", "s_vsc", "s_xld2", "s_xq2", "s_ost2", "s_dio", "s_wv", "s_wq", "s_gu0", "s_gu1", "s_dn0", "s_dn1", "s_gue", "s_dme", "s_wo", "s_oti", "s_oti2", "s_xnl", "s_acc0", "s_wom", "s_cem", "s_xt3", "s_xt4", "s_out", "s_out2", "s_xt5", "s_xt6", "s_out3", "s_out4"]:
            setattr(self, s, self.sem(s))

        self.blk = c.enter_context(nc.Block())
        ctx0 = ExitStack()
        self.ctx0_ps = ExitStack()
        self.emit_init()
        self.emit_p0(ctx0)
        if self.stop_after == "p0":
            self.emit_p0_finish(ctx0)
            self.ctx0_ps.close()
            ctx0.close()
            self.ctx.close()
            return nc
        ctx1 = ExitStack()
        self.ctx0_ps.close()
        self.emit_p1a(ctx0, ctx1)
        ctx0.close()
        self.emit_p1b(ctx1)
        if self.stop_after == "p1":
            self.emit_p1_finish()
            ctx1.close()
            self.ctx.close()
            return nc
        self.ctx_ps_p1b.close()
        ctx1.close()
        ctx2 = ExitStack()
        self.emit_p2(ctx2)
        self.ctx_ps_p2.close()
        ctx2.close()
        self.ctx.close()
        return nc

    def emit_init(self):
        blk = self.blk
        iota_t = self.sb(self.ctx, "iota_t", [128, 128], F32)
        self.iota_t = iota_t

        @blk.gpsimd
        def _(g):
            g.iota(iota_t[:], [[1, 128]], base=0, channel_multiplier=-1,
                   allow_small_or_imprecise_dtypes=True).then_inc(self.s_init, 1)
            g.dma_start(out=self.ecorr_bf[:], in_=self.ecorr[:, :]).then_inc(self.s_gw, 16)
            for ic in range(NIC):
                g.dma_start(out=self.gw_sb.ap()[:, ic, :],
                            in_=self.gwT[ic * 128:(ic + 1) * 128, :]).then_inc(self.s_gw, 16)

        @blk.sync
        def _(s):
            s.dma_start(out=self.sel_sb[:], in_=self.sel_in[:, :]).then_inc(self.s_bo, 16)
            for fc in range(NIC):
                s.dma_start(out=self.bo_sb.ap()[:, fc:fc + 1],
                            in_=self.bo[fc * 128:(fc + 1) * 128, :]).then_inc(self.s_bo, 16)

        @blk.vector
        def _(v):
            v.wait_ge(self.s_init, 1)
            v.tensor_single_scalar(self.ident_f[:], iota_t[:], 0.0, ALU.is_equal
                                   ).then_inc(self.s_init, 1)
            v.wait_ge(self.s_init, 2)
            v.tensor_copy(self.ident_bf[:], self.ident_f[:]).then_inc(self.s_init, 1)
            v.memset(self.ones_r[:], 1.0).then_inc(self.s_init, 1)
            v.memset(self.eps_sb[:], 1e-5).then_inc(self.s_init, 1)
            v.memset(self.negE_sb[:], -4.0).then_inc(self.s_init, 1)
        self.INIT_DONE = 6

    def emit_p0(self, c0):
        blk = self.blk
        self.XnT = self.sb(c0, "XnT", [128, NIC, T], BF16, side="right")
        xt = [self.sb(c0, f"xt{i}", [128, DM], F32, side="right") for i in range(2)]
        xm = self.sb(c0, "xm", [128, DM], F32, side="right")
        sq = self.sb(c0, "sq", [128, DM], F32, side="right")
        s1 = self.sb(c0, "s1", [128, 1], F32, side="right")
        mean = self.sb(c0, "mean", [128, 1], F32, side="right")
        ssum = self.sb(c0, "ssum", [128, 1], F32, side="right")
        sd = self.sb(c0, "sd", [128, 1], F32, side="right")
        rstd = self.sb(c0, "rstd", [128, 1], F32, side="right")
        xn_tm = self.sb(c0, "xn_tm", [128, DM], BF16, side="right")
        tp_ps = [self.ps(self.ctx0_ps, f"tp_ps{i}", [128, 128], BF16) for i in range(2)]
        lg_ps = [self.ps(self.ctx0_ps, f"lg_ps{i}", [128, 32]) for i in range(2)]
        ct_ps = [self.ps(self.ctx0_ps, f"ct_ps{i}", [32, 128], BF16) for i in range(2)]
        ec_ps = self.ps(self.ctx0_ps, "ec_ps", [128, 32])
        scores = self.sb(c0, "scores", [128, 32], F32, side="right")
        sfc = self.sb(c0, "sfc", [128, 32], F32, side="right")
        gs = self.sb(c0, "gs", [128, 8], F32, side="right")
        gt8 = self.sb(c0, "gt8", [128, 8], F32, side="right")
        gzap = self.sb(c0, "gzap", [128, 8], F32, side="right")
        emask = self.sb(c0, "emask", [128, 32], F32, side="right")
        masked = self.sb(c0, "masked", [128, 32], F32, side="right")
        t8 = self.sb(c0, "t8", [128, 8], F32, side="right")
        zap = self.sb(c0, "zap", [128, 32], F32, side="right")
        selm = self.sb(c0, "selm", [128, 32], F32, side="right")
        comb = self.sb(c0, "comb", [128, 32], F32, side="right")
        tsum = self.sb(c0, "tsum", [128, 1], F32, side="right")
        rcpt = self.sb(c0, "rcpt", [128, 1], F32, side="right")
        comb_bf = self.sb(c0, "comb_bf", [128, 32], BF16, side="right")

        @blk.sync
        def _(s):
            for tc in range(NTC):
                if tc >= 2:
                    s.wait_ge(self.s_xtf, tc - 1)
                s.dma_start(out=xt[tc % 2][:], in_=self.x[tc * 128:(tc + 1) * 128, :]
                            ).then_inc(self.s_xld if tc % 2 == 0 else self.s_xld2, 16)

        @blk.vector
        def _(v):
            for tc in range(NTC):
                x_ = xt[tc % 2][:]
                v.wait_ge(self.s_xld if tc % 2 == 0 else self.s_xld2, 16 * (tc // 2 + 1))
                self.vedge(v, v.reduce_sum(s1[:], x_, axis=AX.X))
                self.vedge(v, v.tensor_scalar_mul(mean[:], s1[:], 1.0 / DM))
                if tc >= 1:
                    v.wait_ge(self.s_sq, 2 * tc - 1)
                v.tensor_tensor(xm[:], x_, mean.ap().to_broadcast([128, DM]),
                                ALU.subtract).then_inc(self.s_xtf, 1)
                v.wait_ge(self.s_sq, 2 * tc + 1)
                v.reduce_sum(ssum[:], sq[:], axis=AX.X).then_inc(self.s_lnv, 1)
                v.wait_ge(self.s_sq, 2 * tc + 2)
                self.vedge(v, v.reciprocal(rstd[:], sd[:]))
                if tc >= 1:
                    v.wait_ge(self.s_tmm, 8 * tc)
                v.tensor_tensor(xn_tm[:], xm[:], rstd.ap().to_broadcast([128, DM]),
                                ALU.mult).then_inc(self.s_lnv, 1)

        @blk.scalar
        def _(s):
            for tc in range(NTC):
                s.wait_ge(self.s_xtf, tc + 1)
                s.activation(sq[:], xm[:], AF.Square).then_inc(self.s_sq, 1)
                s.wait_ge(self.s_lnv, 2 * tc + 1)
                if tc == 0:
                    s.wait_ge(self.s_init, 5)
                s.activation(sd[:], ssum[:], AF.Sqrt, bias=self.eps_sb[:, :], scale=1.0 / DM
                             ).then_inc(self.s_sq, 1)
                for fc in range(NIC):
                    n = 8 * tc + fc
                    s.wait_ge(self.s_tmm, n + 1)
                    s.copy(self.XnT.ap()[:, fc, tc * 128:(tc + 1) * 128],
                           tp_ps[n % 2][:, :]).then_inc(self.s_tev, 1)

        @blk.tensor
        def _(t):
            t.wait_ge(self.s_init, self.INIT_DONE)
            for tc in range(NTC):
                t.wait_ge(self.s_lnv, 2 * tc + 2)
                for fc in range(NIC):
                    n = 8 * tc + fc
                    if n >= 2:
                        t.wait_ge(self.s_tev, n - 1)
                    t.transpose(tp_ps[n % 2][:, :],
                                xn_tm.ap()[:, fc * 128:(fc + 1) * 128],
                                self.ident_bf[:]).then_inc(self.s_tmm, 1)

        @blk.gpsimd
        def _(g):
            for qt in range(NQT):
                g.wait_ge(self.s_tev, 8 * 4 * (qt + 1))
                for ic in range(NIC):
                    g.dma_start(out=self.xn_scr[qt, ic, :, :],
                                in_=self.XnT.ap()[:, ic, qt * 512:(qt + 1) * 512]
                                ).then_inc(self.s_scrw, 16)

        @blk.tensor
        def _(t):
            t.wait_ge(self.s_gw, 16 * 9)
            t.matmul(ec_ps[:, :], self.ones_r.ap()[0:1, :], self.ecorr_bf[:],
                     start=True, stop=True).then_inc(self.s_ec, 1)
            for tc in range(NTC):
                t.wait_ge(self.s_tev, 8 * (tc + 1))
                if tc >= 2:
                    t.wait_ge(self.s_sig, tc - 1)
                for ic in range(NIC):
                    i = t.matmul(lg_ps[tc % 2][:, :],
                                 self.XnT.ap()[:, ic, tc * 128:(tc + 1) * 128],
                                 self.gw_sb.ap()[:, ic, :], start=(ic == 0), stop=(ic == 7))
                i.then_inc(self.s_lmm, 1)
                t.wait_ge(self.s_vrt, 3 * tc + 3)
                if tc >= 2:
                    t.wait_ge(self.s_cev, tc - 1)
                t.transpose(ct_ps[tc % 2][:, :], comb_bf[:], self.ident_bf[:]
                            ).then_inc(self.s_cmm, 1)

        @blk.scalar
        def _(s):
            s.wait_ge(self.s_ec, 1)
            s.copy(self.ecorr_bc[:], ec_ps[:, :]).then_inc(self.s_ec, 1)
            for tc in range(NTC):
                s.wait_ge(self.s_lmm, tc + 1)
                if tc >= 1:
                    s.wait_ge(self.s_vrt, 3 * tc)
                s.activation(scores[:], lg_ps[tc % 2][:, :], AF.Sigmoid
                             ).then_inc(self.s_sig, 1)
                s.wait_ge(self.s_cmm, tc + 1)
                s.copy(self.combineT.ap()[:, tc * 128:(tc + 1) * 128],
                       ct_ps[tc % 2][:, :]).then_inc(self.s_cev, 1)

        @blk.vector
        def _(v):
            v.wait_ge(self.s_ec, 2)
            for tc in range(NTC):
                v.wait_ge(self.s_sig, tc + 1)
                E = lambda i: self.vedge(v, i)
                E(v.tensor_add(sfc[:], scores[:], self.ecorr_bc[:]))
                E(v.reduce_max(gs[:], sfc.ap().rearrange("p (g e) -> p g e", e=4),
                               axis=AX.X))
                E(v.max(out=gt8[:], in_=gs[:]))
                v.memset(gt8[:, 4:8], NEG).then_inc(self.s_vrt, 1)
                v.wait_ge(self.s_vrt, 3 * tc + 1)
                E(v.match_replace(out=gzap[:], in_to_replace=gt8[:], in_values=gs[:],
                                  imm_value=NEG))
                E(v.tensor_tensor(gs[:], gzap[:], gs[:], ALU.is_lt))
                for cc in range(4):
                    E(v.tensor_copy(emask.ap()[:, cc:32:4], gs.ap()[:, 0:8]))
                E(v.tensor_mul(masked[:], sfc[:], emask[:]))
                E(v.max(out=t8[:], in_=masked[:]))
                v.memset(t8[:, 6:8], NEG).then_inc(self.s_vrt, 1)
                v.wait_ge(self.s_vrt, 3 * tc + 2)
                E(v.match_replace(out=zap[:], in_to_replace=t8[:], in_values=masked[:],
                                  imm_value=NEG))
                E(v.tensor_tensor(selm[:], zap[:], masked[:], ALU.is_lt))
                E(v.tensor_mul(comb[:], scores[:], selm[:]))
                E(v.reduce_sum(tsum[:], comb[:], axis=AX.X))
                E(v.tensor_scalar_add(rcpt[:], tsum[:], 1e-20))
                E(v.reciprocal(rcpt[:], rcpt[:]))
                E(v.tensor_mul(comb[:], comb[:], rcpt.ap().to_broadcast([128, 32])))
                E(v.tensor_scalar_mul(comb[:], comb[:], 2.5))
                if tc >= 1:
                    v.wait_ge(self.s_cmm, tc)
                v.tensor_copy(comb_bf[:], comb[:]).then_inc(self.s_vrt, 1)

    def emit_p0_finish(self, c0):
        blk = self.blk
        cdbg = self.sb(c0, "cdbg", [32, T], F32)

        @blk.vector
        def _(v):
            v.wait_ge(self.s_cev, NTC)
            v.tensor_copy(cdbg[:], self.combineT[:]).then_inc(self.s_vrt, 1000)

        @blk.sync
        def _(s):
            s.wait_ge(self.s_vrt, 96 + 1000)
            s.dma_start(out=self.comb_dbg[:, :], in_=cdbg[:]).then_inc(self.s_xld, 16)
            s.wait_ge(self.s_xld, 16 * NTC + 16)
            s.wait_ge(self.s_scrw, 16 * NQT * NIC)


    def emit_p1a(self, c0, c1):
        blk = self.blk
        nc = self.nc
        self.c1 = c1
        self.wq_sb = self.sb(c1, "wq_sb", [128, NIC, DM], BF16)
        self.K_all = self.sb(c1, "K_all", [128, 4, T], BF16)
        self.vg = self.sb(c1, "vg", [128, NTC, NKV, 65], BF16)
        self.D = self.sb(c1, "D", [128, DW], F16)
        ctx_w = ExitStack()
        wk_sb = self.sb(ctx_w, "wkv_sb", [128, NIC, 512], BF16)
        wv_sb = wk_sb
        self.ctx_w_p1a = ctx_w
        ctx_ps = ExitStack()
        kp = [self.ps(ctx_ps, f"kp{i}", [128, 512]) for i in range(2)]
        vp = [self.ps(ctx_ps, f"vp{i}", [128, 512]) for i in range(2)]

        @blk.gpsimd
        def _(g):
            g.iota(self.D[:], [[1, DW]], base=-OFF, channel_multiplier=-1,
                   allow_small_or_imprecise_dtypes=True).then_inc(self.s_dio, 1)
            for ic in range(NIC):
                g.dma_start(out=wk_sb.ap()[:, ic, :],
                            in_=self.wkT[ic * 128:(ic + 1) * 128, :]).then_inc(self.s_w1, 16)
            g.wait_ge(self.s_kmm, 32)
            for ic in range(NIC):
                g.dma_start(out=wv_sb.ap()[:, ic, :],
                            in_=self.wvT[ic * 128:(ic + 1) * 128, :]).then_inc(self.s_wv, 16)
            for ic in range(NIC):
                g.dma_start(out=self.wq_sb.ap()[:, ic, :],
                            in_=self.wqT[ic * 128:(ic + 1) * 128, :]).then_inc(self.s_wq, 16)

        @blk.vector
        def _(v):
            v.memset(self.vg.ap()[:, :, :, 64:65], 1.0).then_inc(self.s_vone, 1)

        @blk.tensor
        def _(t):
            t.wait_ge(self.s_cev, NTC)
            t.wait_ge(self.s_tev, NTC * NIC)
            t.wait_ge(self.s_w1, 16 * 8)
            n = 0
            for kvc in range(4):
                for qt in range(NQT):
                    n += 1
                    if n > 2:
                        t.wait_ge(self.s_kev, n - 2)
                    for ic in range(NIC):
                        i = t.matmul(kp[n % 2][:, :],
                                     wk_sb.ap()[:, ic, kvc * 128:(kvc + 1) * 128],
                                     self.XnT.ap()[:, ic, qt * 512:(qt + 1) * 512],
                                     start=(ic == 0), stop=(ic == 7))
                    i.then_inc(self.s_kmm, 1)
            t.wait_ge(self.s_wv, 16 * 8)
            n = 0
            for tc in range(NTC):
                n += 1
                if n > 2:
                    t.wait_ge(self.s_vev, n - 2)
                for ic in range(NIC):
                    i = t.matmul(vp[n % 2][:, :],
                                 self.XnT.ap()[:, ic, tc * 128:(tc + 1) * 128],
                                 wv_sb.ap()[:, ic, :],
                                 start=(ic == 0), stop=(ic == 7))
                i.then_inc(self.s_vmm, 1)

        @blk.scalar
        def _(s):
            s.wait_ge(self.s_dio, 1)
            s.activation(self.D[:], self.D[:], AF.Abs)
            for n in range(1, 33):
                kvc, qt = (n - 1) // NQT, (n - 1) % NQT
                s.wait_ge(self.s_kmm, n)
                s.copy(self.K_all.ap()[:, kvc, qt * 512:(qt + 1) * 512],
                       kp[n % 2][:, :]).then_inc(self.s_kev, 1)

        @blk.vector
        def _(v):
            for n in range(1, 33):
                tc = n - 1
                v.wait_ge(self.s_vmm, n)
                v.tensor_copy(self.vg.ap()[:, tc, :, 0:64],
                              vp[n % 2].ap().rearrange("p (j d) -> p j d", d=64)
                              ).then_inc(self.s_vev, 1)

        self.ctx_ps_p1a = ctx_ps
        ctx_ps.close()
        ctx_w.close()

    def emit_p1b(self, c1):
        blk = self.blk
        nc = self.nc
        Q_sb = [self.sb(c1, f"Q_sb{i}", [128, T], BF16) for i in range(2)]
        xq_sb = [self.sb(c1, f"xq_sb{i}", [128, NIC, 512], BF16) for i in range(2)]
        Bp = self.sb(c1, "Bp", [128, 2, DW], BF16)
        e_sb = [self.sb(c1, f"e_sb{i}", [128, QT1], BF16) for i in range(2)]
        p_sb = [self.sb(c1, f"p_sb{i}", [128, QT1], BF16) for i in range(2)]
        o_nm = [self.sb(c1, f"o_nm{i}", [128, 512], BF16) for i in range(2)]
        rcp_f = self.sb(c1, "rcp_f", [128, QT1], F32)
        rcpb = self.sb(c1, "rcpb", [128, QT1], BF16)
        rbc_sb = self.sb(c1, "rbc_sb", [64, QT1], F32)
        ctx_ps = ExitStack()
        st_ps = [self.ps(ctx_ps, f"st_ps{i}", [128, QT1]) for i in range(2)]
        o2_ps = self.ps(ctx_ps, "o2_ps", [65, QT1])
        rbc_ps = self.ps(ctx_ps, "rbc_ps", [64, 512])
        qp_ps = self.ps(ctx_ps, "qp_ps", [128, 512])
        self.ctx_ps_p1b = ctx_ps

        SLOPES = [2.0 ** (-(h + 1)) for h in range(NH)]

        @blk.sync
        def _(s):
            s.wait_ge(self.s_kev, 32)
            s.wait_ge(self.s_vev, 32)
            for jp in range(NKV):
                for qt in range(NQT):
                    nq = NQT * jp + qt
                    s.wait_ge(self.s_scrw, 16 * NIC * NQT)
                    if nq >= 2:
                        s.wait_ge(self.s_qmm, nq - 1)
                    s.dma_start(out=xq_sb[nq % 2][:],
                                in_=self.xn_scr[qt].transpose([1, 0, 2])
                                ).then_inc(self.s_xq if nq % 2 == 0 else self.s_xq2, 16)

        @blk.gpsimd
        def _(g):
            for jp in range(NKV):
                hA, hB = HEAD_PAIRS[jp]
                for q5 in range(8):
                    m = 8 * jp + q5 + 1
                    qd, co = q5 // 2, (q5 % 2) * 512
                    g.wait_ge(self.s_on, 2 * m)
                    par = self.s_ost if (m - 1) % 2 == 0 else self.s_ost2
                    g.dma_start(out=self.o_scr[qd, hA // 2, (hA % 2) * 64:
                                               (hA % 2) * 64 + 64, co:co + 512],
                                in_=o_nm[(m - 1) % 2].ap()[0:64, :]).then_inc(par, 16)
                    g.dma_start(out=self.o_scr[qd, hB // 2, (hB % 2) * 64:
                                               (hB % 2) * 64 + 64, co:co + 512],
                                in_=o_nm[(m - 1) % 2].ap()[64:128, :]).then_inc(par, 16)

        self._pending_rbc = None

        @blk.tensor
        def _(t):
            t.wait_ge(self.s_kev, 32)
            t.wait_ge(self.s_vev, 32)
            t.wait_ge(self.s_wq, 16 * 8)
            t.wait_ge(self.s_vone, 1)
            step = 0
            for jp in range(NKV):
                if self._pending_rbc:
                    mm_ = self._pending_rbc
                    t.wait_ge(self.s_rbc, 2 * mm_)
                    t.matmul(rbc_ps[:, :], self.ones_r.ap()[64:65, 0:64],
                             rcpb.ap()[64:65, 0:512], start=True, stop=True
                             ).then_inc(self.s_omm, 1)
                    t.wait_ge(self.s_rbe, 2 * mm_ - 1)
                    t.matmul(rbc_ps[:, :], self.ones_r.ap()[64:65, 0:64],
                             rcpb.ap()[64:65, 512:1024], start=True, stop=True
                             ).then_inc(self.s_omm, 1)
                    self._pending_rbc = None
                for qt in range(NQT):
                    nq = NQT * jp + qt
                    if qt == 0:
                        t.wait_ge(self.s_qev, NQT * jp)
                        for _ in range(10):
                            t.matmul(qp_ps.ap()[0:1, 0:64], self.ones_r.ap()[0:1, 0:1],
                                     self.ones_r.ap()[0:1, 0:64], start=True, stop=True)
                    t.wait_ge(self.s_xq if nq % 2 == 0 else self.s_xq2,
                              16 * (nq // 2 + 1))
                    if jp >= 2 and qt == 0:
                        t.wait_ge(self.s_smm, (jp - 1) * 256)
                    if nq >= 1:
                        t.wait_ge(self.s_qev, nq)
                    for ic in range(NIC):
                        i = t.matmul(qp_ps[:, :],
                                     self.wq_sb.ap()[:, ic, jp * 128:(jp + 1) * 128],
                                     xq_sb[nq % 2].ap()[:, ic, :],
                                     start=(ic == 0), stop=(ic == 7))
                    i.then_inc(self.s_qmm, 1)
                hA, hB = HEAD_PAIRS[jp]
                kvA, kvB = hA // 2, hB // 2
                for q5 in range(8):
                    m = 8 * jp + q5 + 1
                    q0 = q5 * 512
                    t.wait_ge(self.s_qev, NQT * (jp + 1))
                    if self._pending_rbc:
                        for _ in range(10):
                            t.matmul(qp_ps.ap()[0:1, 0:64], self.ones_r.ap()[0:1, 0:1],
                                     self.ones_r.ap()[0:1, 0:64], start=True, stop=True)
                    if self._pending_rbc:
                        mm_ = self._pending_rbc
                        t.wait_ge(self.s_rbc, 2 * mm_)
                        t.matmul(rbc_ps[:, :], self.ones_r.ap()[64:65, 0:64],
                                 rcpb.ap()[64:65, 0:512], start=True, stop=True
                                 ).then_inc(self.s_omm, 1)
                        t.wait_ge(self.s_rbe, 2 * mm_ - 1)
                        t.matmul(rbc_ps[:, :], self.ones_r.ap()[64:65, 0:64],
                                 rcpb.ap()[64:65, 512:1024], start=True, stop=True
                                 ).then_inc(self.s_omm, 1)
                        self._pending_rbc = None
                    def emit_Spair(kc, base=None):
                        ns = base + kc + 1
                        if ns > 2:
                            t.wait_ge(self.s_ee, ns - 2)
                        t.matmul(st_ps[ns % 2].ap()[:, 0:512],
                                 self.K_all.ap()[0:64, kvA // 2,
                                                 kc * 128:(kc + 1) * 128],
                                 Q_sb[jp % 2].ap()[0:64, q0:q0 + 512],
                                 start=True, stop=True)
                        i = t.matmul(st_ps[ns % 2].ap()[:, 512:1024],
                                     self.K_all.ap()[64:128, kvB // 2,
                                                     kc * 128:(kc + 1) * 128],
                                     Q_sb[jp % 2].ap()[64:128, q0:q0 + 512],
                                     start=True, stop=True)
                        i.then_inc(self.s_smm, 1)

                    base = step
                    emit_Spair(0, base)
                    emit_Spair(1, base)
                    for kc in range(NKC):
                        step += 1
                        if kc + 2 < NKC:
                            emit_Spair(kc + 2, base)
                        t.matmul(qp_ps.ap()[0:1, 0:64], self.ones_r.ap()[0:1, 0:1],
                                 self.ones_r.ap()[0:1, 0:64], start=True, stop=True)
                        t.wait_ge(self.s_pm, step)
                        if kc == 0:
                            t.wait_ge(self.s_on, 2 * (m - 1))
                        t.matmul(o2_ps.ap()[:, 0:512], self.vg.ap()[:, kc, kvA, :],
                                 p_sb[step % 2].ap()[:, 0:512],
                                 start=(kc == 0), stop=(kc == NKC - 1))
                        i = t.matmul(o2_ps.ap()[:, 512:1024], self.vg.ap()[:, kc, kvB, :],
                                     p_sb[step % 2].ap()[:, 512:1024],
                                     start=(kc == 0), stop=(kc == NKC - 1))
                        i.then_inc(self.s_pvm, 1)
                    self._pending_rbc = m
            if self._pending_rbc:
                mm_ = self._pending_rbc
                t.wait_ge(self.s_rbc, 2 * mm_)
                t.matmul(rbc_ps[:, :], self.ones_r.ap()[64:65, 0:64],
                         rcpb.ap()[64:65, 0:512], start=True, stop=True
                         ).then_inc(self.s_omm, 1)
                t.wait_ge(self.s_rbe, 2 * mm_ - 1)
                t.matmul(rbc_ps[:, :], self.ones_r.ap()[64:65, 0:64],
                         rcpb.ap()[64:65, 512:1024], start=True, stop=True
                         ).then_inc(self.s_omm, 1)
                self._pending_rbc = None

        @blk.scalar
        def _(s):
            s.wait_ge(self.s_init, 6)
            step = 0
            for jp in range(NKV):
                for qt in range(NQT):
                    nq = NQT * jp + qt
                    s.wait_ge(self.s_qmm, nq + 1)
                    s.copy(Q_sb[jp % 2].ap()[:, qt * 512:(qt + 1) * 512],
                           qp_ps[:, :]).then_inc(self.s_qev, 1)
                hA, hB = HEAD_PAIRS[jp]
                if jp >= 1:
                    s.wait_ge(self.s_pm, jp * 256)
                for bi, hh in ((0, hA), (1, hB)):
                    for seg in range(8):
                        s.activation(Bp.ap()[:, bi, seg * 1024:(seg + 1) * 1024],
                                     self.D.ap()[:, seg * 1024:(seg + 1) * 1024],
                                     AF.Exp, bias=self.negE_sb[:, :],
                                     scale=-SLOPES[hh] / 8.0)
                s.nop().then_inc(self.s_bb, 1)
                for q5 in range(8):
                    m = 8 * jp + q5 + 1
                    for kc in range(NKC):
                        step += 1
                        s.wait_ge(self.s_smm, step)
                        if step > 2:
                            s.wait_ge(self.s_pm, step - 2)
                        s.activation(e_sb[step % 2][:], st_ps[step % 2][:, :],
                                     AF.Exp, scale=0.125
                                     ).then_inc(self.s_ee, 1)
                    s.wait_ge(self.s_pvm, step)
                    if m >= 2:
                        s.wait_ge(self.s_omm, 2 * (m - 1))
                    self.sedge(s, s.activation(rcp_f.ap()[64:65, :],
                                               o2_ps.ap()[64:65, :], AF.Ln))
                    s.activation(rcpb.ap()[64:65, :], rcp_f.ap()[64:65, :],
                                 AF.Exp, scale=-1.0).then_inc(self.s_rbc, 2)
                    s.wait_ge(self.s_omm, 2 * m - 1)
                    s.copy(rbc_sb.ap()[:, 0:512], rbc_ps[:, :]).then_inc(self.s_rbe, 1)
                    s.wait_ge(self.s_omm, 2 * m)
                    s.copy(rbc_sb.ap()[:, 512:1024], rbc_ps[:, :]).then_inc(self.s_rbe, 1)

        @blk.vector
        def _(v):
            step = 0
            for jp in range(NKV):
                for q5 in range(8):
                    m = 8 * jp + q5 + 1
                    q0 = q5 * 512
                    for kc in range(NKC):
                        step += 1
                        v.wait_ge(self.s_ee, step)
                        v.wait_ge(self.s_bb, jp + 1)
                        if step > 2:
                            v.wait_ge(self.s_pvm, step - 2)
                        off = q0 - kc * 128 + OFF
                        v.tensor_mul(p_sb[step % 2].ap().rearrange(
                                         "p (b q) -> p b q", b=2),
                                     e_sb[step % 2].ap().rearrange(
                                         "p (b q) -> p b q", b=2),
                                     Bp.ap()[:, :, off:off + 512]
                                     ).then_inc(self.s_pm, 1)
                    v.wait_ge(self.s_rbe, 2 * m)
                    if m >= 3:
                        v.wait_ge(self.s_ost if (m - 1) % 2 == 0 else self.s_ost2,
                                  32 * ((m - 1) // 2))
                    v.tensor_mul(o_nm[(m - 1) % 2].ap()[0:64, :],
                                 o2_ps.ap()[0:64, 0:512], rbc_sb.ap()[:, 0:512])
                    v.tensor_mul(o_nm[(m - 1) % 2].ap()[64:128, :],
                                 o2_ps.ap()[0:64, 512:1024], rbc_sb.ap()[:, 512:1024]
                                 ).then_inc(self.s_on, 2)

    def emit_p1_finish(self):
        blk = self.blk

        @blk.sync
        def _(s):
            s.wait_ge(self.s_ost, 16 * 64)
            s.wait_ge(self.s_ost2, 16 * 64)
            s.wait_ge(self.s_scrw, 16 * NQT * NIC)
            s.wait_ge(self.s_bo, 16 * 9)
        self.ctx_ps_p1b.close()

    def emit_p2(self, c2):
        blk = self.blk
        NEQ = NE * NQ1
        acc = self.sb(c2, "acc", [128, NIC, QT1], F32)
        xnh = self.sb(c2, "xnh", [128, NIC, QT1], BF16)
        gu_sb = [self.sb(c2, f"gu_sb{i}", [128, NIC, 1024], BF16) for i in range(2)]
        dn_sb = [self.sb(c2, f"dn_sb{i}", [128, 4, 1024], BF16) for i in range(2)]
        h_sb = [self.sb(c2, f"h_sb{i}", [128, 4, QT1], BF16) for i in range(2)]
        silu_sb = self.sb(c2, "silu_sb", [128, QT1], BF16)
        ce_bc = [self.sb(c2, f"ce_bc{i}", [128, QT1], BF16) for i in range(2)]
        ot_sb = [self.sb(c2, f"ot_sb{i}", [128, NIC, 512], BF16) for i in range(2)]
        wo_sb = self.sb(c2, "wo_sb", [128, NIC, DM], BF16)
        x_t = [self.sb(c2, f"x_t{i}", [128, DM], F32) for i in range(4)]
        out_sb = [self.sb(c2, f"out_sb{i}", [128, DM], F32) for i in range(4)]
        ctx_ps = ExitStack()
        g_ps = self.ps(ctx_ps, "g_ps", [128, QT1])
        u_ps = self.ps(ctx_ps, "u_ps", [128, QT1])
        dn_ps = [self.ps(ctx_ps, f"dn_ps{i}", [128, 512]) for i in range(2)]
        wo_ps = [self.ps(ctx_ps, f"wo_ps{i}", [128, 512]) for i in range(2)]
        self.ctx_ps_p2 = ctx_ps
        s_gu = [self.s_gu0, self.s_gu1]
        s_dn = [self.s_dn0, self.s_dn1]
        P1_FENCE = [(self.s_on, 2 * 64), (self.s_ost, 16 * 64),
                    (self.s_ost2, 16 * 64), (self.s_pvm, 2048)]

        @blk.gpsimd
        def _(g):
            for sem, val in P1_FENCE:
                g.wait_ge(sem, val)
            g.dma_start(out=wo_sb[:],
                        in_=self.woT.rearrange("(ic p) c -> p ic c", p=128)
                        ).then_inc(self.s_wo, 16)
            for gg in range(NEQ):
                e = gg % NE
                if gg >= 2:
                    g.wait_ge(self.s_gmm, 8 * (gg - 1))
                    g.wait_ge(self.s_umm, 8 * (gg - 1))
                g.dma_start(out=gu_sb[gg % 2][:],
                            in_=self.guT[e].rearrange("(ic p) c -> p ic c", p=128)
                            ).then_inc(s_gu[gg % 2], 16)
                if gg >= 2:
                    g.wait_ge(self.s_dmm, 16 * (gg - 1))
                g.dma_start(out=dn_sb[gg % 2][:],
                            in_=self.dnT[e].rearrange("(cc p) c -> p cc c", p=128)
                            ).then_inc(s_dn[gg % 2], 16)

        @blk.sync
        def _(s):
            for sem, val in P1_FENCE:
                s.wait_ge(sem, val)
            s_xtq = [self.s_xt3, self.s_xt4, self.s_xt5, self.s_xt6]
            s_outq = [self.s_out, self.s_out2, self.s_out3, self.s_out4]

            def emit_output_dmas(q):
                for tc in range(8):
                    nt = 8 * q + tc
                    if nt >= 4:
                        s.wait_ge(self.s_oadd, 2 * nt - 6)
                    s.dma_start(out=x_t[nt % 4][:],
                                in_=self.x[q * QT1 + tc * 128:
                                           q * QT1 + (tc + 1) * 128, :]
                                ).then_inc(s_xtq[nt % 4], 16)
                    s.wait_ge(self.s_oadd, 2 * nt + 2)
                    s.dma_start(out=self.out[q * QT1 + tc * 128:
                                             q * QT1 + (tc + 1) * 128, :],
                                in_=out_sb[nt % 4][:]
                                ).then_inc(s_outq[nt % 4], 16)
            for qd in range(NQ1):
                if qd >= 1:
                    s.wait_ge(self.s_gmm, 8 * NE * qd)
                    s.wait_ge(self.s_umm, 8 * NE * qd)
                for half in range(2):
                    s.dma_start(out=xnh.ap()[:, :, half * 512:(half + 1) * 512],
                                in_=self.xn_scr[2 * qd + half].transpose([1, 0, 2])
                                ).then_inc(self.s_xnl, 16)
                for half in range(2):
                    if qd >= 1:
                        s.wait_ge(self.s_wom, 8 * 2 * qd)
                    s.dma_start(out=ot_sb[half][:],
                                in_=self.o_scr[qd, :, :, half * 512:(half + 1) * 512]
                                .transpose([1, 0, 2])
                                ).then_inc(self.s_oti if half == 0 else self.s_oti2, 16)
                if qd >= 1:
                    emit_output_dmas(qd - 1)
            emit_output_dmas(NQ1 - 1)
            for sq in s_outq:
                s.wait_ge(sq, 16 * 8)

        @blk.tensor
        def _(t):
            t.wait_ge(self.s_wo, 16)
            t.wait_ge(self.s_bo, 16 * 9)
            self._nd = 0
            nh_ = 0
            nsil = 0

            def emit_down(P, k):
                fc, tt = k // 2, k % 2
                self._nd += 1
                nd = self._nd
                if k == 0:
                    t.wait_ge(self.s_hsc, 8 * (P + 1))
                    t.wait_ge(s_dn[P % 2], 16 * (P // 2 + 1))
                if nd > 2:
                    t.wait_ge(self.s_acc, nd - 2)
                for cc in range(4):
                    i = t.matmul(dn_ps[nd % 2][:, :],
                                 dn_sb[P % 2].ap()[:, cc, fc * 128:(fc + 1) * 128],
                                 h_sb[P % 2].ap()[:, cc, tt * 512:(tt + 1) * 512],
                                 start=(cc == 0), stop=(cc == 3))
                i.then_inc(self.s_dmm, 1)

            for qd in range(NQ1):
                for half in range(2):
                    t.wait_ge(self.s_oti if half == 0 else self.s_oti2, 16 * (qd + 1))
                    if qd >= 1 and half == 0:
                        t.wait_ge(self.s_ce, 2 * NRE * qd)
                    for fc in range(NIC):
                        nw = 16 * qd + 8 * half + fc + 1
                        if nw > 2:
                            t.wait_ge(self.s_acc0, nw - 2)
                        for hc in range(NIC):
                            i = t.matmul(wo_ps[nw % 2][:, :],
                                         wo_sb.ap()[:, hc, fc * 128:(fc + 1) * 128],
                                         ot_sb[half].ap()[:, hc, :],
                                         start=(hc == 0), stop=(hc == 7))
                        i.then_inc(self.s_wom, 1)
                pend = None
                for e in range(NE):
                    gg = qd * NE + e
                    routed = e < NRE
                    t.wait_ge(s_gu[gg % 2], 16 * (gg // 2 + 1))
                    t.wait_ge(self.s_xnl, 16 * 2 * (qd + 1))
                    if routed:
                        if e == 0:
                            t.wait_ge(self.s_acc0, 16 * (qd + 1))
                        for half in range(2):
                            t.wait_ge(self.s_hsc, max(0, nh_ - 6))
                            i = t.matmul(wo_ps[half][:, :],
                                         self.sel_sb.ap()[:, e * 128:(e + 1) * 128],
                                         self.combineT.ap()[:, qd * QT1 + half * 512:
                                                            qd * QT1 + (half + 1) * 512],
                                         start=True, stop=True)
                            i.then_inc(self.s_cmm, 1)
                    k = 0
                    for cc in range(4):
                        for op, ps, base in (("g", g_ps, 0), ("u", u_ps, 512)):
                            for tt in range(2):
                                if op == "g":
                                    t.wait_ge(self.s_sil, max(0, nsil - 1))
                                else:
                                    t.wait_ge(self.s_hsc, max(0, nh_ - 1))
                                for ic in range(NIC):
                                    i = t.matmul(ps.ap()[:, tt * 512:(tt + 1) * 512],
                                                 gu_sb[gg % 2].ap()[:, ic,
                                                                    base + cc * 128:
                                                                    base + (cc + 1) * 128],
                                                 xnh.ap()[:, ic, tt * 512:(tt + 1) * 512],
                                                 start=(ic == 0), stop=(ic == 7))
                                i.then_inc(self.s_gmm if op == "g" else self.s_umm, 1)
                                if op == "g":
                                    nsil += 1
                                else:
                                    nh_ += 1
                                k += 1
                    for kk in range(16):
                        emit_down(gg, kk)
                    pend = gg
                t.wait_ge(self.s_acc, self._nd)
                for tc in range(8):
                    nt = 8 * qd + tc
                    for fc in range(NIC):
                        bank = fc // 4
                        ntr = 2 * nt + bank + 1
                        if ntr > 2:
                            t.wait_ge(self.s_oadd, ntr - 2)
                        i = t.transpose(dn_ps[bank].ap()[:, (fc % 4) * 128:
                                                         (fc % 4 + 1) * 128],
                                        acc.ap()[:, fc, tc * 128:(tc + 1) * 128],
                                        self.ident_f[:])
                        if fc % 4 == 3:
                            i.then_inc(self.s_trm, 1)

        @blk.scalar
        def _(s):
            nsil = 0
            for qd in range(NQ1):
                for half in range(2):
                    for fc in range(NIC):
                        nw = 16 * qd + 8 * half + fc + 1
                        s.wait_ge(self.s_wom, nw)
                        s.activation(acc.ap()[:, fc, half * 512:(half + 1) * 512],
                                     wo_ps[nw % 2][:, :], AF.Identity,
                                     bias=self.bo_sb.ap()[:, fc:fc + 1]
                                     ).then_inc(self.s_acc0, 1)
                for e in range(NE):
                    gg = qd * NE + e
                    routed = e < NRE
                    if routed:
                        for half in range(2):
                            s.wait_ge(self.s_cmm, NTC + 2 * (qd * NRE + e) + half + 1)
                            s.copy(ce_bc[gg % 2].ap()[:, half * 512:(half + 1) * 512],
                                   wo_ps[half][:, :]).then_inc(self.s_ce, 1)
                    for cc in range(4):
                        for tt in range(2):
                            nsil += 1
                            s.wait_ge(self.s_gmm, nsil)
                            if nsil > 2:
                                s.wait_ge(self.s_hsc, nsil - 2)
                            s.activation(silu_sb.ap()[:, tt * 512:(tt + 1) * 512],
                                         g_ps.ap()[:, tt * 512:(tt + 1) * 512],
                                         AF.Silu).then_inc(self.s_sil, 1)

        @blk.vector
        def _(v):
            vs_xtq = [self.s_xt3, self.s_xt4, self.s_xt5, self.s_xt6]
            vs_outq = [self.s_out, self.s_out2, self.s_out3, self.s_out4]
            self._vnd = 0
            nh_ = 0

            def emit_add():
                self._vnd += 1
                nd = self._vnd
                fc, tt = ((nd - 1) % 16) // 2, (nd - 1) % 2
                v.wait_ge(self.s_dmm, nd)
                sl = slice(tt * 512, (tt + 1) * 512)
                v.tensor_tensor(acc.ap()[:, fc, sl], dn_ps[nd % 2][:, :],
                                acc.ap()[:, fc, sl], ALU.add
                                ).then_inc(self.s_acc, 1)

            for qd in range(NQ1):
                for e in range(NE):
                    gg = qd * NE + e
                    routed = e < NRE
                    for cc in range(4):
                        for tt in range(2):
                            nh_ += 1
                            sl = slice(tt * 512, (tt + 1) * 512)
                            if cc == 0 and tt == 0 and gg >= 1:
                                v.wait_ge(self.s_dmm, 16 * gg)
                            v.wait_ge(self.s_sil, nh_)
                            v.wait_ge(self.s_umm, nh_)
                            i = v.scalar_tensor_tensor(h_sb[gg % 2].ap()[:, cc, sl],
                                                       u_ps.ap()[:, sl], 1.0,
                                                       silu_sb.ap()[:, sl],
                                                       ALU.mult, ALU.mult)
                            if routed:
                                self.vedge(v, i)
                                if cc == 0 and tt == 0:
                                    v.wait_ge(self.s_ce, 2 * (qd * NRE + e + 1))
                                i = v.tensor_mul(h_sb[gg % 2].ap()[:, cc, sl],
                                                 h_sb[gg % 2].ap()[:, cc, sl],
                                                 ce_bc[gg % 2].ap()[:, sl])
                            i.then_inc(self.s_hsc, 1)
                    for _ in range(16):
                        emit_add()
                for tc in range(8):
                    nt = 8 * qd + tc
                    for bank in range(2):
                        ntr = 2 * nt + bank + 1
                        v.wait_ge(self.s_trm, ntr)
                        v.wait_ge(vs_xtq[nt % 4], 16 * (nt // 4 + 1))
                        if nt >= 4 and bank == 0:
                            v.wait_ge(vs_outq[nt % 4], 16 * (nt // 4))
                        v.tensor_add(out_sb[nt % 4].ap()[:, bank * 512:(bank + 1) * 512],
                                     dn_ps[bank][:, :],
                                     x_t[nt % 4].ap()[:, bank * 512:(bank + 1) * 512]
                                     ).then_inc(self.s_oadd, 1)

import ml_dtypes
from concourse.bass_utils import run_bass_kernel_spmd

_NC_CACHE = {}


def _build_program():
    if "nc" not in _NC_CACHE:
        _NC_CACHE["nc"] = MK(stop_after=None, debug=False).build()
    return _NC_CACHE["nc"]


def kernel(L0_attn_x, L0_attn_Wq, L0_attn_Wk, L0_attn_Wv, L0_attn_Wo, L0_attn_bo,
           L0_ffn_gate_weight, L0_ffn_e_score_correction_bias, L0_ffn_experts_gate_up,
           L0_ffn_experts_down, L0_ffn_shared_gate_w, L0_ffn_shared_up_w,
           L0_ffn_shared_down_w):
    x = np.asarray(L0_attn_x)
    wqT_full = np.asarray(L0_attn_Wq).T.reshape(DM, NH, HD)
    order = [h for pair in HEAD_PAIRS for h in pair]
    wqT = np.ascontiguousarray(wqT_full[:, order, :].reshape(DM, DM))
    wkT = np.ascontiguousarray(np.asarray(L0_attn_Wk).T)
    wvT = np.ascontiguousarray(np.asarray(L0_attn_Wv).T)
    woT = np.ascontiguousarray(np.asarray(L0_attn_Wo).T)
    bo = np.ascontiguousarray(np.asarray(L0_attn_bo).reshape(DM, 1))
    gwT = np.ascontiguousarray(np.asarray(L0_ffn_gate_weight).T)
    ecorr = np.ascontiguousarray(
        np.asarray(L0_ffn_e_score_correction_bias).reshape(1, 32))
    sgw = np.asarray(L0_ffn_shared_gate_w)
    suw = np.asarray(L0_ffn_shared_up_w)
    sdw = np.asarray(L0_ffn_shared_down_w)
    guT = np.concatenate([
        np.ascontiguousarray(np.swapaxes(np.asarray(L0_ffn_experts_gate_up), 1, 2)),
        np.stack([np.concatenate([sgw[512 * j:512 * (j + 1)].T,
                                  suw[512 * j:512 * (j + 1)].T], axis=1)
                  for j in range(2)]),
    ], axis=0).astype(np.float32)
    dnT = np.concatenate([
        np.ascontiguousarray(np.swapaxes(np.asarray(L0_ffn_experts_down), 1, 2)),
        np.stack([np.ascontiguousarray(sdw[:, 512 * j:512 * (j + 1)].T)
                  for j in range(2)]),
    ], axis=0).astype(np.float32)
    sel = np.zeros((32, NE * 128), dtype=ml_dtypes.bfloat16)
    for e in range(32):
        sel[e, 128 * e:128 * (e + 1)] = 1.0

    nc = _build_program()
    in_maps = []
    for core in range(8):
        in_maps.append({
            "x": np.ascontiguousarray(x[core]),
            "wqT": wqT, "wkT": wkT, "wvT": wvT, "woT": woT, "bo": bo,
            "gwT": gwT, "ecorr": ecorr, "sel_in": sel, "guT": guT, "dnT": dnT,
        })
    res = run_bass_kernel_spmd(nc, in_maps, core_ids=list(range(8)))
    out = np.stack([res.results[i]["out"] for i in range(8)], axis=0)
    return out
